# version 05
# baseline (speedup 1.0000x reference)
import numpy as np
import ml_dtypes
import concourse.bass as bass
import concourse.mybir as mybir
from concourse.bass_utils import run_bass_kernel_spmd
from contextlib import ExitStack

F32 = mybir.dt.float32
BF16 = mybir.dt.bfloat16
AF = mybir.ActivationFunctionType
ALU = mybir.AluOpType
AX = mybir.AxisListType

S = 4096
TH = 2048
DM = 2048
FC = 16
NH = 8
NKV = 2
DH = 128
HID = 8192
QT = 512
NQ = S // QT
SM_SCALE = float(1.0 / np.sqrt(128.0))
NEG = -1.0e30
WELEMS = float(HID * DM)

ENGINES = ("sync", "gpsimd", "tensor", "vector", "scalar")


class Prog:
    def __init__(self, nc):
        self.nc = nc
        self.streams = {e: [] for e in ENGINES}
        self.sems = {}
        self.cnt = {}

    def mksem(self, ctx, *names):
        for n in names:
            self.sems[n] = ctx.enter_context(self.nc.semaphore(n))
            self.cnt[n] = 0

    def wait(self, eng, sem, val):
        if val is None or val <= 0:
            return
        h = self.sems[sem]
        self.streams[eng].append(lambda e, h=h, v=val: e.wait_ge(h, v))

    def run(self, eng, build, inc=None, k=1):
        if inc is None:
            self.streams[eng].append(lambda e, b=build: b(e))
            return None
        h = self.sems[inc]
        self.cnt[inc] += k
        self.streams[eng].append(lambda e, b=build, h=h, k=k: b(e).then_inc(h, k))
        return self.cnt[inc]

    def barrier(self):
        self.wait("sync", "lq", self.cnt["lq"])
        self.wait("gpsimd", "lg", self.cnt["lg"])
        for e in ENGINES:
            self.run(e, lambda eng: eng.nop(), inc="bar")
        tgt = self.cnt["bar"]
        for e in ENGINES:
            self.wait(e, "bar", tgt)


def build_program():
    nc = bass.Bass("TRN2", target_bir_lowering=False)

    def din(name, shape, dt=F32):
        return nc.dram_tensor(name, shape, dt, kind="ExternalInput").ap()

    xT = din("xT", [DM, S])
    xownT = din("xownT", [DM, TH])
    wqT = din("wqT", [DM, NH * DH])
    wkT = din("wkT", [DM, NKV * DH])
    wvT = din("wvT", [DM, NKV * DH])
    wgT = din("wgT", [DM, 64])
    woT = din("woT", [DM, DM])
    wupT = din("wupT", [DM, HID])
    wgateT = din("wgateT", [DM, HID])
    wdownT = din("wdownT", [HID, DM])
    cosT = din("cosT", [128, S])
    sinrT = din("sinrT", [128, S])
    masksI = din("masksI", [128, 4 * QT])
    onesI = din("onesI", [128, 8])
    onesbI = din("onesbI", [128, 8], BF16)
    onesrowI = din("onesrowI", [1, 128])
    constsI = din("constsI", [1, 16])

    outT = nc.dram_tensor("outT", [DM, TH], F32, kind="ExternalOutput").ap()

    g_scr = nc.dram_tensor("g_scr", [NH, S], F32).ap()
    o_scr = nc.dram_tensor("o_scr", [2 * NH * DH, TH], BF16).ap()
    o_exch = nc.dram_tensor("o_exch", [4 * NH * DH, TH], BF16).ap()
    xhat_scr = nc.dram_tensor("xhat_scr", [DM, TH], F32).ap()
    tu_scr = nc.dram_tensor("tu_scr", [DM, HID], BF16).ap()
    tg_scr = nc.dram_tensor("tg_scr", [DM, HID], BF16).ap()
    td_scr = nc.dram_tensor("td_scr", [HID, DM], BF16).ap()

    P = Prog(nc)

    with ExitStack() as ctx:
        e_ = ctx.enter_context
        ABF = e_(nc.sbuf_tensor("ABF", [128, 57344], BF16))
        AF32 = e_(nc.sbuf_tensor("AF32", [128, 22528], F32))
        onesf = e_(nc.sbuf_tensor("onesf", [128, 8], F32))
        onesb = e_(nc.sbuf_tensor("onesb", [128, 8], BF16))
        onesrow = e_(nc.sbuf_tensor("onesrow", [1, 128], F32))
        consts = e_(nc.sbuf_tensor("consts", [1, 16], F32))
        fsc = e_(nc.sbuf_tensor("fsc", [128, 8], F32))
        fscv = e_(nc.sbuf_tensor("fscv", [128, 8], F32))
        sc_s = e_(nc.sbuf_tensor("sc_s", [1, 16], F32))
        tau_bc = e_(nc.sbuf_tensor("tau_bc", [128, 8], F32))
        subc = e_(nc.sbuf_tensor("subc", [128, 8], F32))
        pA = e_(nc.psum_tensor("pA", [128, 1024], F32))
        pB = e_(nc.psum_tensor("pB", [128, 1024], F32))
        pC = e_(nc.psum_tensor("pC", [128, 1024], F32))
        pD = e_(nc.psum_tensor("pD", [128, 1024], F32))

        P.mksem(ctx, "lq", "lg", "cc", "pe", "act", "dve", "st", "bar")

        def bf(off, n):
            return ABF[:, off:off + n]

        def f32(off, n):
            return AF32[:, off:off + n]

        def c3(ap, c):
            return ap.rearrange("p (c m) -> p c m", c=c)

        ld1 = P.run("sync", lambda s: s.dma_start(out=onesf[:, :], in_=onesI),
                    inc="lq", k=16)
        ld2 = P.run("sync", lambda s: s.dma_start(out=onesb[:, :], in_=onesbI),
                    inc="lq", k=16)
        ld3 = P.run("sync", lambda s: s.dma_start(out=onesrow[:, :], in_=onesrowI),
                    inc="lq", k=16)
        ld4 = P.run("sync", lambda s: s.dma_start(out=consts[:, :], in_=constsI),
                    inc="lq", k=16)

        WK, WV, WG, XG0, XG1 = 0, 4096, 8192, 9216, 17408
        KALL, VALL = 40960, 49152
        CS0, CS1, QA, QB = 0, 1024, 2048, 2560
        GSLA, GSLB, G8O = 3072, 3584, 4096

        wload = 0
        for off, w, nout in ((WK, wkT, NKV * DH), (WV, wvT, NKV * DH), (WG, wgT, 64)):
            src = w.rearrange("(c p) m -> p c m", p=128)
            wload = P.run("gpsimd", lambda g, o=c3(bf(off, FC * nout), FC), s=src:
                          g.dma_start(out=o, in_=s), inc="lg", k=16)

        xg_done = {}
        cs_done = {}
        consumed_pe = {}
        psq_rot = [pA[:, 0:512], pA[:, 512:1024], pB[:, 0:512], pB[:, 512:1024]]
        psq_free = [0, 0, 0, 0]
        psv_rot = [pC[:, 0:256], pC[:, 512:768]]
        psv_free = [0, 0]
        psg = pD[0:64, 0:512]
        psg_free_act = 0
        rot_i = 0
        vrot_i = 0
        rope_dve = {}
        g_done_dve = 0
        gst = 0

        for sg in range(NQ):
            slot = sg % 2
            xgo = XG0 if slot == 0 else XG1
            if sg >= 2:
                P.wait("gpsimd", "pe", consumed_pe[sg - 2])
            src = xT[:, sg * QT:(sg + 1) * QT].rearrange("(c p) t -> p c t", p=128)
            xg_done[sg] = P.run("gpsimd", lambda g, o=c3(bf(xgo, FC * QT), FC), s=src:
                                g.dma_start(out=o, in_=s), inc="lg", k=16)
            cso = CS0 if slot == 0 else CS1
            if sg >= 2:
                P.wait("sync", "dve", rope_dve.get(sg - 2, 0))
            P.run("sync", lambda s, o=f32(cso, 512), i=cosT[:, sg * QT:(sg + 1) * QT]:
                  s.dma_start(out=o, in_=i), inc="lq", k=16)
            cs_done[sg] = P.run("sync", lambda s, o=f32(cso + 512, 512),
                                i=sinrT[:, sg * QT:(sg + 1) * QT]:
                                s.dma_start(out=o, in_=i), inc="lq", k=16)
            P.wait("tensor", "lg", max(xg_done[sg], wload))
            for kvh in range(NKV):
                ri = rot_i % 4
                ps = psq_rot[ri]
                P.wait("tensor", "dve", psq_free[ri])
                kmm = None
                for fc in range(FC):
                    kmm = P.run("tensor", lambda t, o=ps,
                                l=bf(WK + fc * (NKV * DH) + kvh * DH, DH),
                                r=bf(xgo + fc * QT, QT),
                                st=(fc == 0), sp=(fc == FC - 1):
                                t.matmul(out=o, lhsT=l, rhs=r, start=st, stop=sp),
                                inc="pe" if fc == FC - 1 else None)
                P.wait("vector", "pe", kmm)
                P.wait("vector", "lq", cs_done[sg])
                kslice = bf(KALL + kvh * S + sg * QT, QT)
                P.run("vector", lambda v, o=f32(QA, 512), i0=ps, i1=f32(cso, 512):
                      v.tensor_tensor(out=o, in0=i0, in1=i1, op=ALU.mult))
                P.run("vector", lambda v, o=AF32[0:64, QB:QB + 512], i0=ps[64:128, :],
                      i1=AF32[0:64, cso + 512:cso + 1024]:
                      v.tensor_tensor(out=o, in0=i0, in1=i1, op=ALU.mult))
                P.run("vector", lambda v, o=AF32[64:128, QB:QB + 512], i0=ps[0:64, :],
                      i1=AF32[64:128, cso + 512:cso + 1024]:
                      v.tensor_tensor(out=o, in0=i0, in1=i1, op=ALU.mult))
                kd = P.run("vector", lambda v, o=kslice, i0=f32(QA, 512), i1=f32(QB, 512):
                           v.tensor_tensor(out=o, in0=i0, in1=i1, op=ALU.add), inc="dve")
                psq_free[ri] = kd
                rope_dve[sg] = kd
                rot_i += 1
            for tc in range(4):
                vi = vrot_i % 2
                psv = psv_rot[vi]
                P.wait("tensor", "dve", psv_free[vi])
                vmm = None
                for fc in range(FC):
                    vmm = P.run("tensor", lambda t, o=psv,
                                l=bf(xgo + fc * QT + tc * 128, 128),
                                r=bf(WV + fc * (NKV * DH), NKV * DH),
                                st=(fc == 0), sp=(fc == FC - 1):
                                t.matmul(out=o, lhsT=l, rhs=r, start=st, stop=sp),
                                inc="pe" if fc == FC - 1 else None)
                P.wait("vector", "pe", vmm)
                vd = P.run("vector", lambda v, o=bf(VALL + (sg * 4 + tc) * 256, 256),
                           i=psv: v.tensor_copy(out=o, in_=i), inc="dve")
                psv_free[vi] = vd
                vrot_i += 1
            P.wait("tensor", "act", psg_free_act)
            gmm = None
            for fc in range(FC):
                gmm = P.run("tensor", lambda t, o=psg, l=bf(WG + fc * 64, 64),
                            r=bf(xgo + fc * QT, QT), st=(fc == 0), sp=(fc == FC - 1):
                            t.matmul(out=o, lhsT=l, rhs=r, start=st, stop=sp),
                            inc="pe" if fc == FC - 1 else None)
            P.wait("scalar", "pe", gmm)
            P.wait("scalar", "dve", g_done_dve)
            P.run("scalar", lambda s: s.activation(out=AF32[0:8, GSLA:GSLA + 512], in_=pD[0:8, 0:512],
                                                   func=AF.Sigmoid))
            P.run("scalar", lambda s: s.activation(
                out=AF32[0:8, GSLB:GSLB + 512], in_=pD[32:40, 0:512], func=AF.Sigmoid))
            psg_free_act = P.run("scalar", lambda s: s.activation(out=fsc[:, 0:2], in_=onesf[:, 0:2], func=AF.Square), inc="act")
            P.wait("vector", "act", psg_free_act)
            P.wait("vector", "lq", gst)
            P.run("vector", lambda v: v.tensor_tensor(
                out=AF32[0:8, G8O:G8O + 512], in0=AF32[0:8, GSLA:GSLA + 512],
                in1=AF32[0:8, GSLB:GSLB + 512], op=ALU.add))
            g_done_dve = P.run("vector", lambda v: v.tensor_copy(out=fscv[:, 0:2], in_=onesf[:, 0:2]), inc="dve")
            P.wait("sync", "dve", g_done_dve)
            gst = P.run("sync", lambda s, o=g_scr[:, sg * QT:(sg + 1) * QT]:
                        s.dma_start(out=o, in_=AF32[0:8, G8O:G8O + 512]), inc="lq", k=16)
            consumed_pe[sg] = gmm

        P.barrier()

        WQ, XQ0, XQ1, QTI, PSL, OT = 0, 16384, 24576, 32768, 36864, 38912
        MSK, PCS0, PCS1, PQA, PQB, SMK, NSB, RR, GR = (0, 2048, 3072, 4096, 4608,
                                                       5120, 6144, 6656, 7168)
        GS2 = 7680

        src = wqT.rearrange("(c p) m -> p c m", p=128)
        wqload = P.run("gpsimd", lambda g, o=c3(bf(WQ, FC * NH * DH), FC), s=src:
                       g.dma_start(out=o, in_=s), inc="lg", k=16)
        mload = P.run("sync", lambda s: s.dma_start(out=f32(MSK, 2048), in_=masksI),
                      inc="lq", k=16)
        P.wait("tensor", "lq", max(mload, ld1, ld2, ld3, ld4))
        P.wait("tensor", "lg", wqload)

        xq_done = {}
        pcs_done = {}
        rope_q = [0] * NH
        qbuild_done = {}
        pd_free_dve = 0
        ot_free = [0, 0]
        gsl_done = {}
        exp_cnt = {}
        pvsum_cnt = {}
        smask_free = [0, 0]
        last_exp = [0, 0]
        grd_last = 0
        pair_free_dve = [0, 0]
        bcd = 0

        for q in range(NQ):
            slot = q % 2
            xqo = XQ0 if slot == 0 else XQ1
            if q >= 2:
                P.wait("gpsimd", "pe", qbuild_done[q - 2])
            src = xT[:, q * QT:(q + 1) * QT].rearrange("(c p) t -> p c t", p=128)
            xq_done[q] = P.run("gpsimd", lambda g, o=c3(bf(xqo, FC * QT), FC), s=src:
                               g.dma_start(out=o, in_=s), inc="lg", k=16)
            pcso = PCS0 if slot == 0 else PCS1
            if q >= 2:
                P.wait("sync", "dve", rope_q[NH - 1])
            P.run("sync", lambda s, o=f32(pcso, 512), i=cosT[:, q * QT:(q + 1) * QT]:
                  s.dma_start(out=o, in_=i), inc="lq", k=16)
            pcs_done[q] = P.run("sync", lambda s, o=f32(pcso + 512, 512),
                                i=sinrT[:, q * QT:(q + 1) * QT]:
                                s.dma_start(out=o, in_=i), inc="lq", k=16)
            P.wait("tensor", "lg", xq_done[q])
            for h in range(NH):
                ps = pD[:, (h % 2) * 512:(h % 2) * 512 + 512]
                P.wait("tensor", "dve",
                       max(pd_free_dve, rope_q[h - 2] if h >= 2 else 0))
                qmm = None
                for fc in range(FC):
                    qmm = P.run("tensor", lambda t, o=ps,
                                l=bf(WQ + fc * (NH * DH) + h * DH, DH),
                                r=bf(xqo + fc * QT, QT),
                                st=(fc == 0), sp=(fc == FC - 1):
                                t.matmul(out=o, lhsT=l, rhs=r, start=st, stop=sp),
                                inc="pe" if fc == FC - 1 else None)
                P.wait("vector", "pe", qmm)
                P.wait("vector", "lq", pcs_done[q])
                qsl = bf(QTI + h * 512, 512)
                P.run("vector", lambda v, o=f32(PQA, 512), i0=ps, i1=f32(pcso, 512):
                      v.tensor_tensor(out=o, in0=i0, in1=i1, op=ALU.mult))
                P.run("vector", lambda v, o=AF32[0:64, PQB:PQB + 512], i0=ps[64:128, :],
                      i1=AF32[0:64, pcso + 512:pcso + 1024]:
                      v.tensor_tensor(out=o, in0=i0, in1=i1, op=ALU.mult))
                P.run("vector", lambda v, o=AF32[64:128, PQB:PQB + 512], i0=ps[0:64, :],
                      i1=AF32[64:128, pcso + 512:pcso + 1024]:
                      v.tensor_tensor(out=o, in0=i0, in1=i1, op=ALU.mult))
                rope_q[h] = P.run("vector", lambda v, o=qsl, i0=f32(PQA, 512),
                                  i1=f32(PQB, 512):
                                  v.tensor_tensor(out=o, in0=i0, in1=i1, op=ALU.add),
                                  inc="dve")
            qbuild_done[q] = P.cnt["pe"]

            nkt = q * 4 + 4
            for hp in range(4):
                h0, h1 = 2 * hp, 2 * hp + 1
                kvh = hp // 2
                ps_o = [pA[:, 0:512], pA[:, 512:1024]]
                ps_s = [pB[:, 0:512], pB[:, 512:1024]]
                ps_n = [pC[0:1, 0:512], pC[0:1, 512:1024]]
                P.wait("sync", "dve", grd_last)
                for i, h in enumerate((h0, h1)):
                    gsl_done[(q, h)] = P.run(
                        "sync", lambda s, o=AF32[0:1, GS2 + i * 512:GS2 + i * 512 + 512],
                        ii=g_scr[h:h + 1, q * QT:(q + 1) * QT]:
                        s.dma_start(out=o, in_=ii), inc="lq", k=16)
                vch_prev = None
                for kt in range(nkt):
                    diag = kt - (nkt - 4)
                    kch = bf(KALL + kvh * S + kt * 128, 128)
                    vch = bf(VALL + kt * 256 + kvh * 128, 128)
                    if kt > 0:
                        for i in (0, 1):
                            P.wait("tensor", "act", exp_cnt[(i, kt - 1)])
                            if kt == 1:
                                P.wait("tensor", "dve", pair_free_dve[i])
                            psl = bf(PSL + (i * 2 + ((kt - 1) % 2)) * 512, 512)
                            P.run("tensor", lambda t, o=ps_o[i], l=vch_prev, r=psl,
                                  st=(kt - 1 == 0):
                                  t.matmul(out=o, lhsT=l, rhs=r, start=st, stop=False))
                            pvsum_cnt[(i, kt - 1)] = P.run(
                                "tensor", lambda t, o=ps_n[i], l=onesb[:, 0:1], r=psl,
                                st=(kt - 1 == 0):
                                t.matmul(out=o, lhsT=l, rhs=r, start=st, stop=False),
                                inc="pe")
                    s_done = [0, 0]
                    for i, h in enumerate((h0, h1)):
                        if kt > 0:
                            P.wait("tensor", "act", exp_cnt[(i, kt - 1)])
                        else:
                            P.wait("tensor", "act", last_exp[i])
                            P.wait("tensor", "dve", rope_q[h])
                        s_done[i] = P.run("tensor", lambda t, o=ps_s[i], l=kch,
                                          r=bf(QTI + h * 512, 512):
                                          t.matmul(out=o, lhsT=l, rhs=r, start=True,
                                                   stop=True), inc="pe")
                    mask_done = [0, 0]
                    if diag >= 0:
                        for i in (0, 1):
                            P.wait("vector", "pe", s_done[i])
                            P.wait("vector", "act", smask_free[i])
                            P.run(
                                "vector", lambda v, o=f32(SMK + i * 512, 512), a=ps_s[i],
                                b=f32(MSK + diag * 512, 512):
                                v.tensor_tensor(out=o, in0=a, in1=b, op=ALU.add))
                            mask_done[i] = P.run("vector", lambda v: v.tensor_copy(out=fscv[:, 0:2], in_=onesf[:, 0:2]), inc="dve")
                    for i in (0, 1):
                        psl = bf(PSL + (i * 2 + (kt % 2)) * 512, 512)
                        if kt >= 2:
                            P.wait("scalar", "pe", pvsum_cnt[(i, kt - 2)])
                        elif kt < 2:
                            P.wait("scalar", "pe", pvsum_cnt.get((i, -1), 0))
                        if diag >= 0:
                            P.wait("scalar", "dve", mask_done[i])
                            ecnt = P.run("scalar", lambda s, o=psl,
                                         ii=f32(SMK + i * 512, 512):
                                         s.activation(out=o, in_=ii, func=AF.Exp,
                                                      scale=SM_SCALE), inc="act")
                            smask_free[i] = ecnt
                        else:
                            P.wait("scalar", "pe", s_done[i])
                            ecnt = P.run("scalar", lambda s, o=psl, ii=ps_s[i]:
                                         s.activation(out=o, in_=ii, func=AF.Exp,
                                                      scale=SM_SCALE), inc="act")
                        exp_cnt[(i, kt)] = ecnt
                        last_exp[i] = ecnt
                    vch_prev = vch
                pvsum_cnt[(0, -1)] = None
                pvsum_cnt[(1, -1)] = None
                for i in (0, 1):
                    P.wait("tensor", "act", exp_cnt[(i, nkt - 1)])
                    if nkt == 1:
                        P.wait("tensor", "dve", pair_free_dve[i])
                    psl = bf(PSL + (i * 2 + ((nkt - 1) % 2)) * 512, 512)
                    P.run("tensor", lambda t, o=ps_o[i], l=vch_prev, r=psl,
                          st=(nkt == 1):
                          t.matmul(out=o, lhsT=l, rhs=r, start=st, stop=True))
                    pvsum_cnt[(i, nkt - 1)] = P.run(
                        "tensor", lambda t, o=ps_n[i], l=onesb[:, 0:1], r=psl,
                        st=(nkt == 1):
                        t.matmul(out=o, lhsT=l, rhs=r, start=st, stop=True), inc="pe")
                pvsum_cnt[(0, -1)] = pvsum_cnt[(0, nkt - 1)]
                pvsum_cnt[(1, -1)] = pvsum_cnt[(1, nkt - 1)]
                for i, h in enumerate((h0, h1)):
                    P.wait("vector", "pe", pvsum_cnt[(i, nkt - 1)])
                    P.wait("vector", "pe", bcd)
                    P.wait("vector", "lq", gsl_done[(q, h)])
                    P.run("vector", lambda v, ii=ps_n[i]:
                          v.tensor_copy(out=AF32[0:1, NSB:NSB + 512], in_=ii))
                    P.run("vector", lambda v: v.reciprocal(
                        out=AF32[0:1, RR:RR + 512], in_=AF32[0:1, NSB:NSB + 512]))
                    P.run("vector", lambda v, b=AF32[0:1, GS2 + i * 512:GS2 + i * 512 + 512]:
                          v.tensor_tensor(out=AF32[0:1, GR:GR + 512],
                                          in0=AF32[0:1, RR:RR + 512], in1=b,
                                          op=ALU.mult))
                    grd = P.run("vector", lambda v: v.tensor_copy(out=fscv[:, 0:2], in_=onesf[:, 0:2]), inc="dve")
                    grd_last = grd
                    P.wait("tensor", "dve", max(grd, pd_free_dve))
                    bcd = P.run("tensor", lambda t: t.matmul(
                        out=pD[:, 0:512], lhsT=onesrow[0:1, :],
                        rhs=AF32[0:1, GR:GR + 512], start=True, stop=True), inc="pe")
                    P.wait("vector", "pe", bcd)
                    P.wait("vector", "lq", ot_free[i])
                    otap = bf(OT + i * 512, 512)
                    P.run("vector", lambda v, o=otap, ii=ps_o[i]:
                          v.tensor_copy(out=o, in_=ii))
                    P.run("vector", lambda v, o=otap, a=otap:
                          v.tensor_tensor(out=o, in0=a, in1=pD[:, 0:512],
                                          op=ALU.mult))
                    oscd = P.run("vector", lambda v: v.tensor_copy(out=fscv[:, 0:2], in_=onesf[:, 0:2]), inc="dve")
                    pd_free_dve = oscd
                    pair_free_dve[i] = oscd
                    blk = q // 4
                    dst = o_scr[blk * (NH * DH) + h * DH:blk * (NH * DH) + (h + 1) * DH,
                                (q % 4) * QT:(q % 4 + 1) * QT]
                    P.wait("sync", "dve", oscd)
                    ot_free[i] = P.run("sync", lambda s, o=dst, ii=otap:
                                       s.dma_start(out=o, in_=ii), inc="lq", k=16)

        P.barrier()

        P.run("gpsimd", lambda g: g.collective_compute(
            "AllGather", ALU.bypass,
            replica_groups=[[0, 1], [2, 3], [4, 5], [6, 7]],
            ins=[o_scr], outs=[o_exch]), inc="cc")

        for e in ("sync", "scalar", "tensor", "vector"):
            P.wait(e, "cc", 1)
        SA, SB, T2, ABSB, LOB = 0, 2048, 4096, 6144, 8192
        AE, WT0, WT1, ABSF = 0, 6144, 8192, 10240

        wt_free = {0: ("act", 0), 1: ("act", 0)}
        psum_scale_free = 0
        tau_prev_dve = 0

        def quant_matrix(mi, wsrc, tdst):
            nonlocal psum_scale_free, tau_prev_dve
            abs_cnt = {}
            ssum_cnt = {}
            lo_cnt = {}
            sgn_cnt = {}
            t2_cnt = {}
            st_cnt = {}
            nrow = wsrc.tensor.shape[0] // 128
            for i in range(64):
                rr_, cc4 = i % nrow, i // nrow
                slot = i % 2
                wto = WT0 if slot == 0 else WT1
                tile = wsrc[rr_ * 128:(rr_ + 1) * 128, cc4 * 2048:(cc4 + 1) * 2048]
                sem, val = wt_free[slot]
                P.wait("sync", sem, val)
                ld = P.run("sync", lambda s, o=f32(wto, 2048), ii=tile:
                           s.dma_start(out=o, in_=ii), inc="lq", k=16)
                P.wait("scalar", "lq", ld)
                if i >= 1:
                    P.wait("scalar", "pe", ssum_cnt[i - 1])
                P.run("scalar", lambda s, o=bf(ABSB, 2048), ii=f32(wto, 2048):
                      s.activation(out=o, in_=ii, func=AF.Abs))
                P.run("scalar", lambda s, o=f32(ABSF, 2048), ii=f32(wto, 2048):
                      s.activation(out=o, in_=ii, func=AF.Abs))
                abs_cnt[i] = P.run("scalar", lambda s: s.activation(out=fsc[:, 0:2], in_=onesf[:, 0:2], func=AF.Square), inc="act")
                wt_free[slot] = ("act", abs_cnt[i])
                P.wait("vector", "act", abs_cnt[i])
                if i >= 1:
                    P.wait("vector", "pe", ssum_cnt[i - 1])
                P.run("vector", lambda v: v.tensor_tensor(
                    out=bf(LOB, 2048), in0=f32(ABSF, 2048), in1=bf(ABSB, 2048),
                    op=ALU.subtract))
                lo_cnt[i] = P.run("vector", lambda v: v.tensor_copy(
                    out=fscv[:, 0:2], in_=onesf[:, 0:2]), inc="dve")
                P.wait("tensor", "act", abs_cnt[i])
                P.wait("tensor", "dve", lo_cnt[i])
                if i == 0:
                    P.wait("tensor", "dve", psum_scale_free)
                for j in range(4):
                    psq = (pA if j < 2 else pB)[0:1, (j % 2) * 512:(j % 2) * 512 + 512]
                    P.run("tensor", lambda t, o=psq, l=onesb[:, 0:1],
                          r=bf(ABSB + j * 512, 512), st=(i == 0), sp=(i == 63):
                          t.matmul(out=o, lhsT=l, rhs=r, start=st, stop=sp))
                for j in range(4):
                    psq = (pC if j < 2 else pD)[0:1, (j % 2) * 512:(j % 2) * 512 + 512]
                    ssum_cnt[i] = P.run("tensor", lambda t, o=psq, l=onesb[:, 0:1],
                                        r=bf(LOB + j * 512, 512), st=(i == 0),
                                        sp=(i == 63):
                                        t.matmul(out=o, lhsT=l, rhs=r, start=st, stop=sp),
                                        inc="pe" if j == 3 else None)
            P.wait("vector", "pe", ssum_cnt[63])
            P.run("vector", lambda v: v.tensor_copy(out=AF32[0:1, AE:AE + 1024],
                                                    in_=pA[0:1, :]))
            P.run("vector", lambda v: v.tensor_copy(
                out=AF32[0:1, AE + 1024:AE + 2048], in_=pB[0:1, :]))
            P.run("vector", lambda v: v.tensor_copy(
                out=AF32[0:1, AE + 2048:AE + 3072], in_=pC[0:1, :]))
            pcopy = P.run("vector", lambda v: v.tensor_copy(
                out=AF32[0:1, AE + 3072:AE + 4096], in_=pD[0:1, :]), inc="dve")
            psum_scale_free = pcopy
            P.run("vector", lambda v: v.tensor_tensor(
                out=AF32[0:1, AE:AE + 1024], in0=AF32[0:1, AE:AE + 1024],
                in1=AF32[0:1, AE + 1024:AE + 2048], op=ALU.add))
            P.run("vector", lambda v: v.tensor_tensor(
                out=AF32[0:1, AE + 2048:AE + 3072], in0=AF32[0:1, AE + 2048:AE + 3072],
                in1=AF32[0:1, AE + 3072:AE + 4096], op=ALU.add))
            P.run("vector", lambda v: v.tensor_tensor(
                out=AF32[0:1, AE:AE + 1024], in0=AF32[0:1, AE:AE + 1024],
                in1=AF32[0:1, AE + 2048:AE + 3072], op=ALU.add))
            tcd = P.run("vector", lambda v: v.reduce_sum(
                out=AF32[0:1, AE + 4096:AE + 4097], in_=AF32[0:1, AE:AE + 1024],
                axis=AX.X), inc="dve")
            P.wait("tensor", "dve", max(tcd, tau_prev_dve))
            P.run("tensor", lambda t: t.matmul(
                out=pC[0:1, 0:4], lhsT=AF32[0:1, AE + 4096:AE + 4097],
                rhs=consts[0:1, 0:4], start=True, stop=True))
            tbr = P.run("tensor", lambda t: t.matmul(
                out=pC[0:1, 4:6], lhsT=AF32[0:1, AE + 4096:AE + 4097],
                rhs=consts[0:1, 4:6], start=True, stop=True), inc="pe")
            P.wait("vector", "pe", tbr)
            P.run("vector", lambda v: v.tensor_tensor(
                out=sc_s[0:1, 4 * mi:4 * mi + 4], in0=pC[0:1, 0:4],
                in1=consts[0:1, 6:10], op=ALU.add))
            trd = P.run("vector", lambda v: v.tensor_tensor(
                out=AF32[0:1, AE + 4104:AE + 4106], in0=pC[0:1, 4:6],
                in1=consts[0:1, 10:12], op=ALU.add), inc="dve")
            P.wait("tensor", "dve", trd)
            tbc = P.run("tensor", lambda t: t.matmul(
                out=pC[:, 0:2], lhsT=onesrow[0:1, :],
                rhs=AF32[0:1, AE + 4104:AE + 4106], start=True, stop=True), inc="pe")
            P.wait("vector", "pe", tbc)
            tbcd = P.run("vector", lambda v: v.tensor_copy(out=tau_bc[:, 0:2],
                                                           in_=pC[:, 0:2]), inc="dve")
            tau_prev_dve = tbcd
            for i in range(64):
                rr_, cc4 = i % nrow, i // nrow
                slot = i % 2
                wto = WT0 if slot == 0 else WT1
                tile = wsrc[rr_ * 128:(rr_ + 1) * 128, cc4 * 2048:(cc4 + 1) * 2048]
                sem, val = wt_free[slot]
                P.wait("sync", sem, val)
                ld = P.run("sync", lambda s, o=f32(wto, 2048), ii=tile:
                           s.dma_start(out=o, in_=ii), inc="lq", k=16)
                P.wait("scalar", "lq", ld)
                if i == 0:
                    P.wait("scalar", "dve", tbcd)
                if i >= 1:
                    P.wait("scalar", "dve", t2_cnt[i - 1])
                P.run("scalar", lambda s, o=bf(SA, 2048), ii=f32(wto, 2048):
                      s.activation(out=o, in_=ii, func=AF.Sign, bias=tau_bc[:, 0:1]))
                P.run("scalar", lambda s, o=bf(SB, 2048), ii=f32(wto, 2048):
                      s.activation(out=o, in_=ii, func=AF.Sign, bias=tau_bc[:, 1:2]))
                sgn_cnt[i] = P.run("scalar", lambda s: s.activation(out=fsc[:, 0:2], in_=onesf[:, 0:2], func=AF.Square), inc="act")
                wt_free[slot] = ("act", sgn_cnt[i])
                P.wait("vector", "act", sgn_cnt[i])
                if i >= 1:
                    P.wait("vector", "lq", st_cnt[i - 1])
                P.run("vector", lambda v: v.tensor_tensor(
                    out=bf(T2, 2048), in0=bf(SA, 2048), in1=bf(SB, 2048), op=ALU.add))
                t2_cnt[i] = P.run("vector", lambda v: v.tensor_copy(out=fscv[:, 0:2], in_=onesf[:, 0:2]), inc="dve")
                P.wait("sync", "dve", t2_cnt[i])
                dsttile = tdst[rr_ * 128:(rr_ + 1) * 128, cc4 * 2048:(cc4 + 1) * 2048]
                st_cnt[i] = P.run("sync", lambda s, o=dsttile:
                                  s.dma_start(out=o, in_=bf(T2, 2048)), inc="lq", k=16)

        quant_matrix(0, wupT, tu_scr)
        quant_matrix(1, wgateT, tg_scr)
        quant_matrix(2, wdownT, td_scr)
        sgd = P.run("vector", lambda v: v.tensor_tensor(
            out=sc_s[0:1, 12:14], in0=sc_s[0:1, 5:7], in1=sc_s[0:1, 9:11],
            op=ALU.mult), inc="dve")
        P.wait("tensor", "dve", max(sgd, tau_prev_dve))
        P.run("tensor", lambda t: t.matmul(
            out=pC[:, 0:2], lhsT=onesrow[0:1, :], rhs=sc_s[0:1, 1:3],
            start=True, stop=True))
        sbc = P.run("tensor", lambda t: t.matmul(
            out=pC[:, 2:4], lhsT=onesrow[0:1, :], rhs=sc_s[0:1, 12:14],
            start=True, stop=True), inc="pe")
        P.wait("vector", "pe", sbc)
        P.run("vector", lambda v: v.tensor_copy(out=subc[:, 0:4], in_=pC[:, 0:4]),
              inc="dve")

        P.barrier()

        WOH, OMV, SQ3, R1B = 0, 16384, 32768, 33792
        SE3, XO, TT3, R1 = 0, 4096, 5120, 6144

        par_holder = {}

        P.wait("sync", "cc", 1)

        s2mm = 0
        xh_last = 0
        for grp in range(2):
            gc0 = grp * 1024
            if grp == 1:
                P.wait("sync", "pe", P.cnt["pe"])
            omv_ld = 0
            for r in range(2):
                def mk(s, r=r, grp=grp):
                    if "v" not in par_holder:
                        par_holder["v"] = s.partition_id() % 2
                    par = par_holder["v"]
                    srcap = o_exch[bass.ds((r * 2 + par) * 1024, 1024),
                                   grp * 1024:(grp + 1) * 1024]
                    srcap = srcap.rearrange("(c p) t -> p c t", p=128)
                    return s.dma_start(out=c3(bf(OMV + r * 8 * 1024, 8 * 1024), 8),
                                       in_=srcap)
                omv_ld = P.run("sync", mk, inc="lq", k=16)
            r1d = 0
            r1d_oc = {}
            sq_free_pe = 0
            for half in range(2):
                src = woT[:, half * 1024:(half + 1) * 1024].rearrange(
                    "(c p) m -> p c m", p=128)
                if grp + half > 0:
                    P.wait("gpsimd", "pe", P.cnt["pe"])
                wold = P.run("gpsimd", lambda g, o=c3(bf(WOH, FC * 1024), FC), s=src:
                             g.dma_start(out=o, in_=s), inc="lg", k=16)
                P.wait("tensor", "lg", wold)
                P.wait("tensor", "lq", omv_ld)
                for oc2 in range(8):
                    oc = half * 8 + oc2
                    psw = pA if oc % 2 == 0 else pD
                    P.wait("sync", "dve", r1d)
                    xold = P.run("sync", lambda s, o=f32(XO, 1024),
                                 ii=xownT[oc * 128:(oc + 1) * 128, gc0:gc0 + 1024]:
                                 s.dma_start(out=o, in_=ii), inc="lq", k=16)
                    P.wait("tensor", "dve", max(r1d_oc.get(oc - 2, 0), xh_last))
                    wmm = None
                    for hcg in range(16):
                        for hf in range(2):
                            wmm = P.run("tensor", lambda t,
                                        o=psw[:, hf * 512:hf * 512 + 512],
                                        l=bf(WOH + hcg * 1024 + oc2 * 128, 128),
                                        r=bf(OMV + hcg * 1024 + hf * 512, 512),
                                        st=(hcg == 0), sp=(hcg == 15):
                                        t.matmul(out=o, lhsT=l, rhs=r, start=st, stop=sp),
                                        inc="pe" if (hcg == 15 and hf == 1) else None)
                    P.wait("vector", "pe", wmm)
                    P.wait("vector", "lq", xold)
                    r1sl = f32(R1 + oc * 1024, 1024)
                    P.run("vector", lambda v, o=r1sl, pp=psw:
                          v.tensor_tensor(out=o, in0=pp[:, :], in1=f32(XO, 1024),
                                          op=ALU.add))
                    r1d = P.run("vector", lambda v: v.tensor_copy(out=fscv[:, 0:2], in_=onesf[:, 0:2]), inc="dve")
                    r1d_oc[oc] = r1d
                    P.wait("scalar", "dve", r1d)
                    P.wait("scalar", "pe", sq_free_pe)
                    P.wait("vector", "pe", sq_free_pe)
                    P.run("vector", lambda v, o=bf(R1B, 1024), ii=r1sl:
                          v.tensor_copy(out=o, in_=ii))
                    cst = P.run("vector", lambda v: v.tensor_copy(out=fscv[:, 0:2], in_=onesf[:, 0:2]), inc="dve")
                    P.run("scalar", lambda s, o=bf(SQ3, 1024), ii=r1sl:
                          s.activation(out=o, in_=ii, func=AF.Square))
                    sqd = P.run("scalar", lambda s: s.activation(out=fsc[:, 0:2], in_=onesf[:, 0:2], func=AF.Square), inc="act")
                    P.wait("tensor", "act", sqd)
                    P.wait("tensor", "dve", cst)
                    for hf in range(2):
                        P.run("tensor", lambda t, o=pB[0:1, hf * 512:hf * 512 + 512],
                              l=onesb[:, 0:1], r=bf(R1B + hf * 512, 512),
                              st=(oc == 0), sp=(oc == 15):
                              t.matmul(out=o, lhsT=l, rhs=r, start=st, stop=sp))
                    for hf in range(2):
                        s2mm = P.run("tensor", lambda t,
                                     o=pC[0:1, hf * 512:hf * 512 + 512],
                                     l=onesb[:, 0:1], r=bf(SQ3 + hf * 512, 512),
                                     st=(oc == 0), sp=(oc == 15):
                                     t.matmul(out=o, lhsT=l, rhs=r, start=st, stop=sp),
                                     inc="pe" if hf == 1 else None)
                    sq_free_pe = s2mm
            mu = AF32[0:1, SE3:SE3 + 1024]
            var = AF32[0:1, SE3 + 1024:SE3 + 2048]
            m2 = AF32[0:1, SE3 + 2048:SE3 + 3072]
            rstd = AF32[0:1, SE3 + 3072:SE3 + 4096]
            P.wait("vector", "pe", s2mm)
            P.run("vector", lambda v: v.tensor_copy(out=mu, in_=pB[0:1, :]))
            P.run("vector", lambda v: v.tensor_copy(out=var, in_=pC[0:1, :]))
            P.run("vector", lambda v: v.tensor_scalar(
                out=mu, in0=mu, scalar1=float(1.0 / DM), scalar2=None, op0=ALU.mult))
            P.run("vector", lambda v: v.tensor_scalar(
                out=var, in0=var, scalar1=float(1.0 / DM), scalar2=None, op0=ALU.mult))
            P.run("vector", lambda v: v.tensor_tensor(out=m2, in0=mu, in1=mu,
                                                      op=ALU.mult))
            P.run("vector", lambda v: v.tensor_tensor(
                out=var, in0=var, in1=m2, op=ALU.subtract))
            vard = P.run("vector", lambda v: v.tensor_scalar(
                out=var, in0=var, scalar1=1e-5, scalar2=None, op0=ALU.add), inc="dve")
            P.wait("scalar", "dve", vard)
            P.run("scalar", lambda s: s.activation(out=rstd, in_=var, func=AF.Ln))
            P.run("scalar", lambda s: s.activation(out=rstd, in_=rstd, func=AF.Exp,
                                                   scale=-0.5))
            rsd = P.run("scalar", lambda s: s.activation(out=fsc[:, 0:2], in_=onesf[:, 0:2], func=AF.Square), inc="act")
            P.wait("tensor", "act", rsd)
            for hf in range(2):
                P.run("tensor", lambda t, o=pD[:, hf * 512:hf * 512 + 512],
                      l=onesrow[0:1, :],
                      r=AF32[0:1, SE3 + hf * 512:SE3 + hf * 512 + 512]:
                      t.matmul(out=o, lhsT=l, rhs=r, start=True, stop=True))
            bcd2 = None
            for hf in range(2):
                bcd2 = P.run("tensor", lambda t, o=pB[:, hf * 512:hf * 512 + 512],
                             l=onesrow[0:1, :],
                             r=AF32[0:1, SE3 + 3072 + hf * 512:SE3 + 3072 + hf * 512 + 512]:
                             t.matmul(out=o, lhsT=l, rhs=r, start=True, stop=True),
                             inc="pe" if hf == 1 else None)
            P.wait("vector", "pe", bcd2)
            for oc in range(16):
                r1sl = f32(R1 + oc * 1024, 1024)
                P.run("vector", lambda v, a=r1sl:
                      v.tensor_tensor(out=f32(TT3, 1024), in0=a, in1=pD[:, :],
                                      op=ALU.subtract))
                P.run("vector", lambda v, o=r1sl:
                      v.tensor_tensor(out=o, in0=f32(TT3, 1024), in1=pB[:, :],
                                      op=ALU.mult))
                xhd = P.run("vector", lambda v: v.tensor_copy(out=fscv[:, 0:2],
                                                              in_=onesf[:, 0:2]),
                            inc="dve")
                xh_last = xhd
            P.wait("sync", "dve", xh_last)
            for oc in range(16):
                r1sl = f32(R1 + oc * 1024, 1024)
                P.run("sync", lambda s, o=xhat_scr[oc * 128:(oc + 1) * 128,
                                                   gc0:gc0 + 1024], ii=r1sl:
                      s.dma_start(out=o, in_=ii), inc="lq", k=16)

        P.barrier()

        XH, HB, TU0, TU1, TG0, TG1, TD = 0, 16384, 24576, 28672, 32768, 36864, 40960
        SE5, T5, SIL, ACC = 0, 4096, 5120, 6144

        up_cons = {}
        dmm_last = 0
        hdone = 0
        silu_free = 0
        accd = 0
        od_last = 0
        for grp in range(2):
            gc0 = grp * 1024
            if grp == 1:
                P.wait("sync", "dve", od_last)
            accld = P.run("sync", lambda s, o=c3(f32(ACC, 16384), FC),
                          ii=xhat_scr[:, gc0:gc0 + 1024].rearrange(
                              "(c p) t -> p c t", p=128):
                          s.dma_start(out=o, in_=ii), inc="lq", k=16)
            if grp == 1:
                P.wait("gpsimd", "pe", up_cons[31])
            xhld = P.run("gpsimd", lambda g, o=c3(bf(XH, 16384), FC),
                         ii=xhat_scr[:, gc0:gc0 + 1024].rearrange(
                             "(c p) t -> p c t", p=128):
                         g.dma_start(out=o, in_=ii), inc="lg", k=16)
            P.wait("tensor", "lg", xhld)
            P.wait("vector", "lq", accld)
            for seg in range(8):
                td_done = 0
                for hc in range(8):
                    P.wait("sync", "pe", dmm_last)
                    td_done = P.run(
                        "sync", lambda s, o=bf(TD + hc * 2048, 2048),
                        ii=td_scr[(seg * 8 + hc) * 128:(seg * 8 + hc + 1) * 128, :]:
                        s.dma_start(out=o, in_=ii), inc="lq", k=16)
                for lg in range(4):
                    slot = lg % 2
                    tuo = TU0 if slot == 0 else TU1
                    tgo = TG0 if slot == 0 else TG1
                    h0c = seg * 1024 + lg * 256
                    li = seg * 4 + lg
                    if li >= 2:
                        P.wait("sync", "pe", up_cons[li - 2])
                    P.run("sync", lambda s, o=c3(bf(tuo, 4096), FC),
                          ii=tu_scr[:, h0c:h0c + 256].rearrange(
                              "(c p) m -> p c m", p=128):
                          s.dma_start(out=o, in_=ii), inc="lq", k=16)
                    tgd = P.run("sync", lambda s, o=c3(bf(tgo, 4096), FC),
                                ii=tg_scr[:, h0c:h0c + 256].rearrange(
                                    "(c p) m -> p c m", p=128):
                                s.dma_start(out=o, in_=ii), inc="lq", k=16)
                    P.wait("tensor", "lq", tgd)
                    for hh in range(2):
                        P.wait("tensor", "act", silu_free)
                        P.wait("tensor", "dve", max(hdone, od_last))
                        umm = None
                        for fc in range(FC):
                            for hf in range(2):
                                umm = P.run("tensor", lambda t,
                                            o=pA[:, hf * 512:hf * 512 + 512],
                                            l=bf(tuo + fc * 256 + hh * 128, 128),
                                            r=bf(XH + fc * 1024 + hf * 512, 512),
                                            st=(fc == 0), sp=(fc == FC - 1):
                                            t.matmul(out=o, lhsT=l, rhs=r, start=st,
                                                     stop=sp),
                                            inc="pe" if (fc == FC - 1 and hf == 1)
                                            else None)
                        gmm5 = None
                        for fc in range(FC):
                            for hf in range(2):
                                gmm5 = P.run("tensor", lambda t,
                                             o=pB[:, hf * 512:hf * 512 + 512],
                                             l=bf(tgo + fc * 256 + hh * 128, 128),
                                             r=bf(XH + fc * 1024 + hf * 512, 512),
                                             st=(fc == 0), sp=(fc == FC - 1):
                                             t.matmul(out=o, lhsT=l, rhs=r, start=st,
                                                      stop=sp),
                                             inc="pe" if (fc == FC - 1 and hf == 1)
                                             else None)
                        P.wait("scalar", "pe", umm)
                        P.wait("scalar", "dve", hdone)
                        P.run("scalar", lambda s: s.activation(
                            out=f32(SIL, 1024), in_=pA[:, :], func=AF.Silu,
                            scale=subc[:, 0:1]))
                        silu_free = P.run("scalar", lambda s: s.activation(out=fsc[:, 0:2], in_=onesf[:, 0:2], func=AF.Square), inc="act")
                        P.wait("vector", "act", silu_free)
                        P.wait("vector", "pe", gmm5)
                        hidx = lg * 2 + hh
                        P.run("vector", lambda v, o=bf(HB + hidx * 1024, 1024):
                              v.tensor_tensor(out=o, in0=f32(SIL, 1024),
                                              in1=pB[:, :], op=ALU.mult))
                        hdone = P.run("vector", lambda v: v.tensor_copy(out=fscv[:, 0:2], in_=onesf[:, 0:2]), inc="dve")
                    up_cons[li] = gmm5
                P.wait("tensor", "dve", hdone)
                P.wait("tensor", "lq", td_done)
                for dc in range(16):
                    P.wait("tensor", "dve", accd)
                    dmm = None
                    for hc in range(8):
                        for hf in range(2):
                            dmm = P.run("tensor", lambda t,
                                        o=pC[:, hf * 512:hf * 512 + 512],
                                        l=bf(TD + hc * 2048 + dc * 128, 128),
                                        r=bf(HB + hc * 1024 + hf * 512, 512),
                                        st=(hc == 0), sp=(hc == 7):
                                        t.matmul(out=o, lhsT=l, rhs=r, start=st, stop=sp),
                                        inc="pe" if (hc == 7 and hf == 1) else None)
                    P.wait("vector", "pe", dmm)
                    P.run("vector", lambda v, o=f32(ACC + dc * 1024, 1024),
                          b=f32(ACC + dc * 1024, 1024):
                          v.scalar_tensor_tensor(out=o, in0=pC[:, :],
                                                 scalar=subc[:, 2:3], in1=b,
                                                 op0=ALU.mult, op1=ALU.add))
                    accd = P.run("vector", lambda v: v.tensor_copy(out=fscv[:, 0:2], in_=onesf[:, 0:2]), inc="dve")
                dmm_last = dmm
            mu = AF32[0:1, SE5:SE5 + 1024]
            var = AF32[0:1, SE5 + 1024:SE5 + 2048]
            m2 = AF32[0:1, SE5 + 2048:SE5 + 3072]
            rstd = AF32[0:1, SE5 + 3072:SE5 + 4096]
            sq_free_pe = 0
            s2mm5 = 0
            P.wait("scalar", "dve", accd)
            for oc in range(16):
                asl = f32(ACC + oc * 1024, 1024)
                P.wait("tensor", "dve", accd)
                P.wait("scalar", "pe", sq_free_pe)
                P.wait("vector", "pe", sq_free_pe)
                P.wait("vector", "dve", accd)
                P.run("vector", lambda v, o=bf(HB + 1024, 1024), ii=asl:
                      v.tensor_copy(out=o, in_=ii))
                cst = P.run("vector", lambda v: v.tensor_copy(out=fscv[:, 0:2], in_=onesf[:, 0:2]), inc="dve")
                P.run("scalar", lambda s, o=bf(HB, 1024), ii=asl:
                      s.activation(out=o, in_=ii, func=AF.Square))
                sqd = P.run("scalar", lambda s: s.activation(out=fsc[:, 0:2], in_=onesf[:, 0:2], func=AF.Square), inc="act")
                P.wait("tensor", "act", sqd)
                P.wait("tensor", "dve", cst)
                for hf in range(2):
                    P.run("tensor", lambda t, o=pB[0:1, hf * 512:hf * 512 + 512],
                          l=onesb[:, 0:1], r=bf(HB + 1024 + hf * 512, 512),
                          st=(oc == 0), sp=(oc == 15):
                          t.matmul(out=o, lhsT=l, rhs=r, start=st, stop=sp))
                for hf in range(2):
                    s2mm5 = P.run("tensor", lambda t,
                                  o=pD[0:1, hf * 512:hf * 512 + 512],
                                  l=onesb[:, 0:1], r=bf(HB + hf * 512, 512),
                                  st=(oc == 0), sp=(oc == 15):
                                  t.matmul(out=o, lhsT=l, rhs=r, start=st, stop=sp),
                                  inc="pe" if hf == 1 else None)
                sq_free_pe = s2mm5
            P.wait("vector", "pe", s2mm5)
            P.run("vector", lambda v: v.tensor_copy(out=mu, in_=pB[0:1, :]))
            P.run("vector", lambda v: v.tensor_copy(out=var, in_=pD[0:1, :]))
            P.run("vector", lambda v: v.tensor_scalar(
                out=mu, in0=mu, scalar1=float(1.0 / DM), scalar2=None, op0=ALU.mult))
            P.run("vector", lambda v: v.tensor_scalar(
                out=var, in0=var, scalar1=float(1.0 / DM), scalar2=None, op0=ALU.mult))
            P.run("vector", lambda v: v.tensor_tensor(out=m2, in0=mu, in1=mu,
                                                      op=ALU.mult))
            P.run("vector", lambda v: v.tensor_tensor(
                out=var, in0=var, in1=m2, op=ALU.subtract))
            vard = P.run("vector", lambda v: v.tensor_scalar(
                out=var, in0=var, scalar1=1e-5, scalar2=None, op0=ALU.add), inc="dve")
            P.wait("scalar", "dve", vard)
            P.run("scalar", lambda s: s.activation(out=rstd, in_=var, func=AF.Ln))
            P.run("scalar", lambda s: s.activation(out=rstd, in_=rstd, func=AF.Exp,
                                                   scale=-0.5))
            rsd = P.run("scalar", lambda s: s.activation(out=fsc[:, 0:2], in_=onesf[:, 0:2], func=AF.Square), inc="act")
            P.wait("tensor", "act", rsd)
            for hf in range(2):
                P.run("tensor", lambda t, o=pA[:, hf * 512:hf * 512 + 512],
                      l=onesrow[0:1, :],
                      r=AF32[0:1, SE5 + hf * 512:SE5 + hf * 512 + 512]:
                      t.matmul(out=o, lhsT=l, rhs=r, start=True, stop=True))
            bcd3 = None
            for hf in range(2):
                bcd3 = P.run("tensor", lambda t, o=pB[:, hf * 512:hf * 512 + 512],
                             l=onesrow[0:1, :],
                             r=AF32[0:1, SE5 + 3072 + hf * 512:
                                    SE5 + 3072 + hf * 512 + 512]:
                             t.matmul(out=o, lhsT=l, rhs=r, start=True, stop=True),
                             inc="pe" if hf == 1 else None)
            P.wait("vector", "pe", bcd3)
            for oc in range(16):
                asl = f32(ACC + oc * 1024, 1024)
                P.run("vector", lambda v, a=asl:
                      v.tensor_tensor(out=f32(T5, 1024), in0=a, in1=pA[:, :],
                                      op=ALU.subtract))
                P.run("vector", lambda v, o=asl:
                      v.tensor_tensor(out=o, in0=f32(T5, 1024), in1=pB[:, :],
                                      op=ALU.mult))
                od = P.run("vector", lambda v: v.tensor_copy(out=fscv[:, 0:2],
                                                             in_=onesf[:, 0:2]),
                           inc="dve")
                od_last = od
            P.wait("sync", "dve", od_last)
            for oc in range(16):
                asl = f32(ACC + oc * 1024, 1024)
                P.run("sync", lambda s, o=outT[oc * 128:(oc + 1) * 128, gc0:gc0 + 1024],
                      ii=asl: s.dma_start(out=o, in_=ii), inc="st", k=16)

        P.wait("sync", "st", P.cnt["st"])

        with nc.Block() as block:
            for eng in ENGINES:
                def body(e, eng=eng):
                    for fn in P.streams[eng]:
                        fn(e)
                getattr(block, eng)(body)

    return nc


def _tables():
    t = np.arange(S, dtype=np.float32)
    inv = (1.0 / np.power(10000.0, np.arange(0, DH, 2, dtype=np.float64) / DH)
           ).astype(np.float32)
    freqs = (t[:, None] * inv[None, :]).astype(np.float32)
    cos64 = np.cos(freqs.astype(np.float64)).astype(np.float32)
    sin64 = np.sin(freqs.astype(np.float64)).astype(np.float32)
    cosT = np.concatenate([cos64.T, cos64.T], axis=0)
    sinrT = np.concatenate([-sin64.T, sin64.T], axis=0)
    masks = np.zeros((128, 4 * QT), dtype=np.float32)
    tm = np.arange(128)[:, None]
    ql = np.arange(QT)[None, :]
    for j in range(4):
        masks[:, j * QT:(j + 1) * QT] = np.where(j * 128 + tm <= ql, 0.0, NEG)
    return cosT, sinrT, masks


_NC_CACHE = {}


def kernel(L0_attn_hidden_states, L0_attn_Wq, L0_attn_Wk, L0_attn_Wv, L0_attn_Wo,
           L0_attn_Wkc, L0_attn_Wgslc, L0_attn_Wgswa, L0_ffn_W_up, L0_ffn_W_gate,
           L0_ffn_W_down):
    x = np.asarray(L0_attn_hidden_states, dtype=np.float32)
    Wq = np.asarray(L0_attn_Wq, dtype=np.float32)
    Wk = np.asarray(L0_attn_Wk, dtype=np.float32)
    Wv = np.asarray(L0_attn_Wv, dtype=np.float32)
    Wo = np.asarray(L0_attn_Wo, dtype=np.float32)
    Wgslc = np.asarray(L0_attn_Wgslc, dtype=np.float32)
    Wgswa = np.asarray(L0_attn_Wgswa, dtype=np.float32)
    Wu = np.asarray(L0_ffn_W_up, dtype=np.float32)
    Wg = np.asarray(L0_ffn_W_gate, dtype=np.float32)
    Wd = np.asarray(L0_ffn_W_down, dtype=np.float32)

    cosT, sinrT, masks = _tables()
    onesI = np.zeros((128, 8), dtype=np.float32)
    onesI[:, 0] = 1.0
    onesbI = np.zeros((128, 8), dtype=ml_dtypes.bfloat16)
    onesbI[:, 0] = 1.0
    onesrowI = np.ones((1, 128), dtype=np.float32)
    constsI = np.zeros((1, 16), dtype=np.float32)
    constsI[0, 0] = 1.0 / WELEMS
    constsI[0, 1] = 0.5 / WELEMS
    constsI[0, 4] = -0.5 / WELEMS
    constsI[0, 5] = 0.5 / WELEMS
    constsI[0, 6] = 1e-8
    constsI[0, 7] = 5e-9
    constsI[0, 10] = -5e-9
    constsI[0, 11] = 5e-9

    C = np.ascontiguousarray
    wupT = C(Wu.T)
    wgateT = C(Wg.T)
    wdownT = C(Wd.T)
    woT = C(Wo.T)

    in_maps = []
    for core in range(8):
        be, par = core // 2, core % 2
        heads = slice(par * 8 * DH, (par + 1) * 8 * DH)
        kvh = slice(par * 2 * DH, (par + 1) * 2 * DH)
        wgTl = np.zeros((DM, 64), dtype=np.float32)
        wgTl[:, 0:8] = Wgslc[par * 8:(par + 1) * 8, :].T
        wgTl[:, 32:40] = Wgswa[par * 8:(par + 1) * 8, :].T
        in_maps.append({
            "xT": C(x[be].T),
            "xownT": C(x[be, par * TH:(par + 1) * TH, :].T),
            "wqT": C(Wq[heads, :].T),
            "wkT": C(Wk[kvh, :].T),
            "wvT": C(Wv[kvh, :].T),
            "wgT": wgTl,
            "woT": woT,
            "wupT": wupT,
            "wgateT": wgateT,
            "wdownT": wdownT,
            "cosT": cosT,
            "sinrT": sinrT,
            "masksI": masks,
            "onesI": onesI,
            "onesbI": onesbI,
            "onesrowI": onesrowI,
            "constsI": constsI,
        })

    if "nc" not in _NC_CACHE:
        _NC_CACHE["nc"] = build_program()
    nc = _NC_CACHE["nc"]
    res = run_bass_kernel_spmd(nc, in_maps, core_ids=list(range(8)))
    out = np.empty((4, S, DM), dtype=np.float32)
    for core in range(8):
        be, par = core // 2, core % 2
        out[be, par * TH:(par + 1) * TH, :] = np.asarray(
            res.results[core]["outT"], dtype=np.float32).T
    return out


# version 06
# speedup vs baseline: 1.0015x; 1.0015x over previous
import numpy as np
import ml_dtypes
import concourse.bass as bass
import concourse.mybir as mybir
from concourse.bass_utils import run_bass_kernel_spmd
from contextlib import ExitStack

F32 = mybir.dt.float32
BF16 = mybir.dt.bfloat16
AF = mybir.ActivationFunctionType
ALU = mybir.AluOpType
AX = mybir.AxisListType

S = 4096
TH = 2048
DM = 2048
FC = 16
NH = 8
NKV = 2
DH = 128
HID = 8192
QT = 512
NQ = S // QT
SM_SCALE = float(1.0 / np.sqrt(128.0))
NEG = -1.0e30
WELEMS = float(HID * DM)

ENGINES = ("sync", "gpsimd", "tensor", "vector", "scalar")


class Prog:
    def __init__(self, nc):
        self.nc = nc
        self.streams = {e: [] for e in ENGINES}
        self.sems = {}
        self.cnt = {}

    def mksem(self, ctx, *names):
        for n in names:
            self.sems[n] = ctx.enter_context(self.nc.semaphore(n))
            self.cnt[n] = 0

    def wait(self, eng, sem, val):
        if val is None or val <= 0:
            return
        h = self.sems[sem]
        self.streams[eng].append(lambda e, h=h, v=val: e.wait_ge(h, v))

    def run(self, eng, build, inc=None, k=1):
        if inc is None:
            self.streams[eng].append(lambda e, b=build: b(e))
            return None
        h = self.sems[inc]
        self.cnt[inc] += k
        self.streams[eng].append(lambda e, b=build, h=h, k=k: b(e).then_inc(h, k))
        return self.cnt[inc]

    def barrier(self):
        self.wait("sync", "lq", self.cnt["lq"])
        self.wait("gpsimd", "lg", self.cnt["lg"])
        for e in ENGINES:
            self.run(e, lambda eng: eng.nop(), inc="bar")
        tgt = self.cnt["bar"]
        for e in ENGINES:
            self.wait(e, "bar", tgt)


def build_program():
    nc = bass.Bass("TRN2", target_bir_lowering=False)

    def din(name, shape, dt=F32):
        return nc.dram_tensor(name, shape, dt, kind="ExternalInput").ap()

    xT = din("xT", [DM, S])
    xownT = din("xownT", [DM, TH])
    wqT = din("wqT", [DM, NH * DH])
    wkT = din("wkT", [DM, NKV * DH])
    wvT = din("wvT", [DM, NKV * DH])
    wgT = din("wgT", [DM, 64])
    woT = din("woT", [DM, DM])
    wupT = din("wupT", [DM, HID])
    wgateT = din("wgateT", [DM, HID])
    wdownT = din("wdownT", [HID, DM])
    cosT = din("cosT", [128, S])
    sinrT = din("sinrT", [128, S])
    masksI = din("masksI", [128, 4 * QT])
    onesI = din("onesI", [128, 8])
    onesbI = din("onesbI", [128, 8], BF16)
    onesrowI = din("onesrowI", [1, 128])
    constsI = din("constsI", [1, 16])

    outT = nc.dram_tensor("outT", [DM, TH], F32, kind="ExternalOutput").ap()

    g_scr = nc.dram_tensor("g_scr", [NH, S], F32).ap()
    o_scr = nc.dram_tensor("o_scr", [2 * NH * DH, TH], BF16).ap()
    o_exch = nc.dram_tensor("o_exch", [4 * NH * DH, TH], BF16).ap()
    xhat_scr = nc.dram_tensor("xhat_scr", [DM, TH], F32).ap()
    tu_scr = nc.dram_tensor("tu_scr", [DM, HID], BF16).ap()
    tg_scr = nc.dram_tensor("tg_scr", [DM, HID], BF16).ap()
    td_scr = nc.dram_tensor("td_scr", [HID, DM], BF16).ap()

    P = Prog(nc)

    with ExitStack() as ctx:
        e_ = ctx.enter_context
        ABF = e_(nc.sbuf_tensor("ABF", [128, 57344], BF16))
        AF32 = e_(nc.sbuf_tensor("AF32", [128, 22528], F32))
        onesf = e_(nc.sbuf_tensor("onesf", [128, 8], F32))
        onesb = e_(nc.sbuf_tensor("onesb", [128, 8], BF16))
        onesrow = e_(nc.sbuf_tensor("onesrow", [1, 128], F32))
        consts = e_(nc.sbuf_tensor("consts", [1, 16], F32))
        fsc = e_(nc.sbuf_tensor("fsc", [128, 8], F32))
        fscv = e_(nc.sbuf_tensor("fscv", [128, 8], F32))
        sc_s = e_(nc.sbuf_tensor("sc_s", [1, 16], F32))
        tau_bc = e_(nc.sbuf_tensor("tau_bc", [128, 8], F32))
        subc = e_(nc.sbuf_tensor("subc", [128, 8], F32))
        pA = e_(nc.psum_tensor("pA", [128, 1024], F32))
        pB = e_(nc.psum_tensor("pB", [128, 1024], F32))
        pC = e_(nc.psum_tensor("pC", [128, 1024], F32))
        pD = e_(nc.psum_tensor("pD", [128, 1024], F32))

        P.mksem(ctx, "lq", "lg", "cc", "pe", "act", "dve", "st", "bar")

        def bf(off, n):
            return ABF[:, off:off + n]

        def f32(off, n):
            return AF32[:, off:off + n]

        def c3(ap, c):
            return ap.rearrange("p (c m) -> p c m", c=c)

        ld1 = P.run("sync", lambda s: s.dma_start(out=onesf[:, :], in_=onesI),
                    inc="lq", k=16)
        ld2 = P.run("sync", lambda s: s.dma_start(out=onesb[:, :], in_=onesbI),
                    inc="lq", k=16)
        ld3 = P.run("sync", lambda s: s.dma_start(out=onesrow[:, :], in_=onesrowI),
                    inc="lq", k=16)
        ld4 = P.run("sync", lambda s: s.dma_start(out=consts[:, :], in_=constsI),
                    inc="lq", k=16)

        WK, WV, WG, XG0, XG1 = 0, 4096, 8192, 9216, 17408
        KALL, VALL = 40960, 49152
        CS0, CS1, QA, QB = 0, 1024, 2048, 2560
        GSLA, GSLB, G8O = 3072, 3584, 4096

        wload = 0
        for off, w, nout in ((WK, wkT, NKV * DH), (WV, wvT, NKV * DH), (WG, wgT, 64)):
            src = w.rearrange("(c p) m -> p c m", p=128)
            wload = P.run("gpsimd", lambda g, o=c3(bf(off, FC * nout), FC), s=src:
                          g.dma_start(out=o, in_=s), inc="lg", k=16)

        xg_done = {}
        cs_done = {}
        consumed_pe = {}
        psq_rot = [pA[:, 0:512], pA[:, 512:1024], pB[:, 0:512], pB[:, 512:1024]]
        psq_free = [0, 0, 0, 0]
        psv_rot = [pC[:, 0:256], pC[:, 512:768]]
        psv_free = [0, 0]
        psg = pD[0:64, 0:512]
        psg_free_act = 0
        rot_i = 0
        vrot_i = 0
        rope_dve = {}
        g_done_dve = 0
        gst = 0

        for sg in range(NQ):
            slot = sg % 2
            xgo = XG0 if slot == 0 else XG1
            if sg >= 2:
                P.wait("gpsimd", "pe", consumed_pe[sg - 2])
            src = xT[:, sg * QT:(sg + 1) * QT].rearrange("(c p) t -> p c t", p=128)
            xg_done[sg] = P.run("gpsimd", lambda g, o=c3(bf(xgo, FC * QT), FC), s=src:
                                g.dma_start(out=o, in_=s), inc="lg", k=16)
            cso = CS0 if slot == 0 else CS1
            if sg >= 2:
                P.wait("sync", "dve", rope_dve.get(sg - 2, 0))
            P.run("sync", lambda s, o=f32(cso, 512), i=cosT[:, sg * QT:(sg + 1) * QT]:
                  s.dma_start(out=o, in_=i), inc="lq", k=16)
            cs_done[sg] = P.run("sync", lambda s, o=f32(cso + 512, 512),
                                i=sinrT[:, sg * QT:(sg + 1) * QT]:
                                s.dma_start(out=o, in_=i), inc="lq", k=16)
            P.wait("tensor", "lg", max(xg_done[sg], wload))
            for kvh in range(NKV):
                ri = rot_i % 4
                ps = psq_rot[ri]
                P.wait("tensor", "dve", psq_free[ri])
                kmm = None
                for fc in range(FC):
                    kmm = P.run("tensor", lambda t, o=ps,
                                l=bf(WK + fc * (NKV * DH) + kvh * DH, DH),
                                r=bf(xgo + fc * QT, QT),
                                st=(fc == 0), sp=(fc == FC - 1):
                                t.matmul(out=o, lhsT=l, rhs=r, start=st, stop=sp),
                                inc="pe" if fc == FC - 1 else None)
                P.wait("vector", "pe", kmm)
                P.wait("vector", "lq", cs_done[sg])
                kslice = bf(KALL + kvh * S + sg * QT, QT)
                P.run("vector", lambda v, o=f32(QA, 512), i0=ps, i1=f32(cso, 512):
                      v.tensor_tensor(out=o, in0=i0, in1=i1, op=ALU.mult))
                P.run("vector", lambda v, o=AF32[0:64, QB:QB + 512], i0=ps[64:128, :],
                      i1=AF32[0:64, cso + 512:cso + 1024]:
                      v.tensor_tensor(out=o, in0=i0, in1=i1, op=ALU.mult))
                P.run("vector", lambda v, o=AF32[64:128, QB:QB + 512], i0=ps[0:64, :],
                      i1=AF32[64:128, cso + 512:cso + 1024]:
                      v.tensor_tensor(out=o, in0=i0, in1=i1, op=ALU.mult))
                kd = P.run("vector", lambda v, o=kslice, i0=f32(QA, 512), i1=f32(QB, 512):
                           v.tensor_tensor(out=o, in0=i0, in1=i1, op=ALU.add), inc="dve")
                psq_free[ri] = kd
                rope_dve[sg] = kd
                rot_i += 1
            for tc in range(4):
                vi = vrot_i % 2
                psv = psv_rot[vi]
                P.wait("tensor", "dve", psv_free[vi])
                vmm = None
                for fc in range(FC):
                    vmm = P.run("tensor", lambda t, o=psv,
                                l=bf(xgo + fc * QT + tc * 128, 128),
                                r=bf(WV + fc * (NKV * DH), NKV * DH),
                                st=(fc == 0), sp=(fc == FC - 1):
                                t.matmul(out=o, lhsT=l, rhs=r, start=st, stop=sp),
                                inc="pe" if fc == FC - 1 else None)
                P.wait("vector", "pe", vmm)
                vd = P.run("vector", lambda v, o=bf(VALL + (sg * 4 + tc) * 256, 256),
                           i=psv: v.tensor_copy(out=o, in_=i), inc="dve")
                psv_free[vi] = vd
                vrot_i += 1
            P.wait("tensor", "act", psg_free_act)
            gmm = None
            for fc in range(FC):
                gmm = P.run("tensor", lambda t, o=psg, l=bf(WG + fc * 64, 64),
                            r=bf(xgo + fc * QT, QT), st=(fc == 0), sp=(fc == FC - 1):
                            t.matmul(out=o, lhsT=l, rhs=r, start=st, stop=sp),
                            inc="pe" if fc == FC - 1 else None)
            P.wait("scalar", "pe", gmm)
            P.wait("scalar", "dve", g_done_dve)
            P.run("scalar", lambda s: s.activation(out=AF32[0:8, GSLA:GSLA + 512], in_=pD[0:8, 0:512],
                                                   func=AF.Sigmoid))
            P.run("scalar", lambda s: s.activation(
                out=AF32[0:8, GSLB:GSLB + 512], in_=pD[32:40, 0:512], func=AF.Sigmoid))
            psg_free_act = P.run("scalar", lambda s: s.activation(out=fsc[:, 0:2], in_=onesf[:, 0:2], func=AF.Square), inc="act")
            P.wait("vector", "act", psg_free_act)
            P.wait("vector", "lq", gst)
            P.run("vector", lambda v: v.tensor_tensor(
                out=AF32[0:8, G8O:G8O + 512], in0=AF32[0:8, GSLA:GSLA + 512],
                in1=AF32[0:8, GSLB:GSLB + 512], op=ALU.add))
            g_done_dve = P.run("vector", lambda v: v.tensor_copy(out=fscv[:, 0:2], in_=onesf[:, 0:2]), inc="dve")
            P.wait("sync", "dve", g_done_dve)
            gst = P.run("sync", lambda s, o=g_scr[:, sg * QT:(sg + 1) * QT]:
                        s.dma_start(out=o, in_=AF32[0:8, G8O:G8O + 512]), inc="lq", k=16)
            consumed_pe[sg] = gmm

        P.barrier()

        WQ, XQ0, XQ1, QTI, PSL, OT = 0, 16384, 24576, 32768, 36864, 38912
        MSK, PCS0, PCS1, PQA, PQB, SMK, NSB, RR, GR = (0, 2048, 3072, 4096, 4608,
                                                       5120, 6144, 6656, 7168)
        GS2 = 7680

        src = wqT.rearrange("(c p) m -> p c m", p=128)
        wqload = P.run("gpsimd", lambda g, o=c3(bf(WQ, FC * NH * DH), FC), s=src:
                       g.dma_start(out=o, in_=s), inc="lg", k=16)
        mload = P.run("sync", lambda s: s.dma_start(out=f32(MSK, 2048), in_=masksI),
                      inc="lq", k=16)
        P.wait("tensor", "lq", max(mload, ld1, ld2, ld3, ld4))
        P.wait("tensor", "lg", wqload)

        xq_done = {}
        pcs_done = {}
        rope_q = [0] * NH
        qbuild_done = {}
        pd_free_dve = 0
        ot_free = [0, 0]
        gsl_done = {}
        exp_cnt = {}
        pvsum_cnt = {}
        smask_free = [0, 0]
        last_exp = [0, 0]
        grd_last = 0
        pair_free_dve = [0, 0]
        bcd = 0

        for q in range(NQ):
            slot = q % 2
            xqo = XQ0 if slot == 0 else XQ1
            if q >= 2:
                P.wait("gpsimd", "pe", qbuild_done[q - 2])
            src = xT[:, q * QT:(q + 1) * QT].rearrange("(c p) t -> p c t", p=128)
            xq_done[q] = P.run("gpsimd", lambda g, o=c3(bf(xqo, FC * QT), FC), s=src:
                               g.dma_start(out=o, in_=s), inc="lg", k=16)
            pcso = PCS0 if slot == 0 else PCS1
            if q >= 2:
                P.wait("sync", "dve", rope_q[NH - 1])
            P.run("sync", lambda s, o=f32(pcso, 512), i=cosT[:, q * QT:(q + 1) * QT]:
                  s.dma_start(out=o, in_=i), inc="lq", k=16)
            pcs_done[q] = P.run("sync", lambda s, o=f32(pcso + 512, 512),
                                i=sinrT[:, q * QT:(q + 1) * QT]:
                                s.dma_start(out=o, in_=i), inc="lq", k=16)
            P.wait("tensor", "lg", xq_done[q])
            for h in range(NH):
                ps = pD[:, (h % 2) * 512:(h % 2) * 512 + 512]
                P.wait("tensor", "dve",
                       max(pd_free_dve, rope_q[h - 2] if h >= 2 else 0))
                qmm = None
                for fc in range(FC):
                    qmm = P.run("tensor", lambda t, o=ps,
                                l=bf(WQ + fc * (NH * DH) + h * DH, DH),
                                r=bf(xqo + fc * QT, QT),
                                st=(fc == 0), sp=(fc == FC - 1):
                                t.matmul(out=o, lhsT=l, rhs=r, start=st, stop=sp),
                                inc="pe" if fc == FC - 1 else None)
                P.wait("vector", "pe", qmm)
                P.wait("vector", "lq", pcs_done[q])
                qsl = bf(QTI + h * 512, 512)
                P.run("vector", lambda v, o=f32(PQA, 512), i0=ps, i1=f32(pcso, 512):
                      v.tensor_tensor(out=o, in0=i0, in1=i1, op=ALU.mult))
                P.run("vector", lambda v, o=AF32[0:64, PQB:PQB + 512], i0=ps[64:128, :],
                      i1=AF32[0:64, pcso + 512:pcso + 1024]:
                      v.tensor_tensor(out=o, in0=i0, in1=i1, op=ALU.mult))
                P.run("vector", lambda v, o=AF32[64:128, PQB:PQB + 512], i0=ps[0:64, :],
                      i1=AF32[64:128, pcso + 512:pcso + 1024]:
                      v.tensor_tensor(out=o, in0=i0, in1=i1, op=ALU.mult))
                rope_q[h] = P.run("vector", lambda v, o=qsl, i0=f32(PQA, 512),
                                  i1=f32(PQB, 512):
                                  v.tensor_tensor(out=o, in0=i0, in1=i1, op=ALU.add),
                                  inc="dve")
            qbuild_done[q] = P.cnt["pe"]

            nkt = q * 4 + 4
            for hp in range(4):
                h0, h1 = 2 * hp, 2 * hp + 1
                kvh = hp // 2
                ps_o = [pA[:, 0:512], pA[:, 512:1024]]
                ps_s = [pB[:, 0:512], pB[:, 512:1024]]
                ps_n = [pC[0:1, 0:512], pC[0:1, 512:1024]]
                P.wait("sync", "dve", grd_last)
                for i, h in enumerate((h0, h1)):
                    gsl_done[(q, h)] = P.run(
                        "sync", lambda s, o=AF32[0:1, GS2 + i * 512:GS2 + i * 512 + 512],
                        ii=g_scr[h:h + 1, q * QT:(q + 1) * QT]:
                        s.dma_start(out=o, in_=ii), inc="lq", k=16)
                vch_prev = None
                for kt in range(nkt):
                    diag = kt - (nkt - 4)
                    kch = bf(KALL + kvh * S + kt * 128, 128)
                    vch = bf(VALL + kt * 256 + kvh * 128, 128)
                    if kt > 0:
                        for i in (0, 1):
                            P.wait("tensor", "act", exp_cnt[(i, kt - 1)])
                            if kt == 1:
                                P.wait("tensor", "dve", pair_free_dve[i])
                            psl = bf(PSL + (i * 2 + ((kt - 1) % 2)) * 512, 512)
                            P.run("tensor", lambda t, o=ps_o[i], l=vch_prev, r=psl,
                                  st=(kt - 1 == 0):
                                  t.matmul(out=o, lhsT=l, rhs=r, start=st, stop=False))
                            pvsum_cnt[(i, kt - 1)] = P.run(
                                "tensor", lambda t, o=ps_n[i], l=onesb[:, 0:1], r=psl,
                                st=(kt - 1 == 0):
                                t.matmul(out=o, lhsT=l, rhs=r, start=st, stop=False),
                                inc="pe")
                    s_done = [0, 0]
                    for i, h in enumerate((h0, h1)):
                        if kt > 0:
                            P.wait("tensor", "act", exp_cnt[(i, kt - 1)])
                        else:
                            P.wait("tensor", "act", last_exp[i])
                            P.wait("tensor", "dve", rope_q[h])
                        s_done[i] = P.run("tensor", lambda t, o=ps_s[i], l=kch,
                                          r=bf(QTI + h * 512, 512):
                                          t.matmul(out=o, lhsT=l, rhs=r, start=True,
                                                   stop=True), inc="pe")
                    mask_done = [0, 0]
                    if diag >= 0:
                        for i in (0, 1):
                            P.wait("vector", "pe", s_done[i])
                            P.wait("vector", "act", smask_free[i])
                            P.run(
                                "vector", lambda v, o=f32(SMK + i * 512, 512), a=ps_s[i],
                                b=f32(MSK + diag * 512, 512):
                                v.tensor_tensor(out=o, in0=a, in1=b, op=ALU.add))
                            mask_done[i] = P.run("vector", lambda v: v.tensor_copy(out=fscv[:, 0:2], in_=onesf[:, 0:2]), inc="dve")
                    for i in (0, 1):
                        psl = bf(PSL + (i * 2 + (kt % 2)) * 512, 512)
                        if kt >= 2:
                            P.wait("scalar", "pe", pvsum_cnt[(i, kt - 2)])
                        elif kt < 2:
                            P.wait("scalar", "pe", pvsum_cnt.get((i, -1), 0))
                        if diag >= 0:
                            P.wait("scalar", "dve", mask_done[i])
                            ecnt = P.run("scalar", lambda s, o=psl,
                                         ii=f32(SMK + i * 512, 512):
                                         s.activation(out=o, in_=ii, func=AF.Exp,
                                                      scale=SM_SCALE), inc="act")
                            smask_free[i] = ecnt
                        else:
                            P.wait("scalar", "pe", s_done[i])
                            ecnt = P.run("scalar", lambda s, o=psl, ii=ps_s[i]:
                                         s.activation(out=o, in_=ii, func=AF.Exp,
                                                      scale=SM_SCALE), inc="act")
                        exp_cnt[(i, kt)] = ecnt
                        last_exp[i] = ecnt
                    vch_prev = vch
                pvsum_cnt[(0, -1)] = None
                pvsum_cnt[(1, -1)] = None
                for i in (0, 1):
                    P.wait("tensor", "act", exp_cnt[(i, nkt - 1)])
                    if nkt == 1:
                        P.wait("tensor", "dve", pair_free_dve[i])
                    psl = bf(PSL + (i * 2 + ((nkt - 1) % 2)) * 512, 512)
                    P.run("tensor", lambda t, o=ps_o[i], l=vch_prev, r=psl,
                          st=(nkt == 1):
                          t.matmul(out=o, lhsT=l, rhs=r, start=st, stop=True))
                    pvsum_cnt[(i, nkt - 1)] = P.run(
                        "tensor", lambda t, o=ps_n[i], l=onesb[:, 0:1], r=psl,
                        st=(nkt == 1):
                        t.matmul(out=o, lhsT=l, rhs=r, start=st, stop=True), inc="pe")
                pvsum_cnt[(0, -1)] = pvsum_cnt[(0, nkt - 1)]
                pvsum_cnt[(1, -1)] = pvsum_cnt[(1, nkt - 1)]
                for i, h in enumerate((h0, h1)):
                    P.wait("vector", "pe", pvsum_cnt[(i, nkt - 1)])
                    P.wait("vector", "pe", bcd)
                    P.wait("vector", "lq", gsl_done[(q, h)])
                    P.run("vector", lambda v, ii=ps_n[i]:
                          v.tensor_copy(out=AF32[0:1, NSB:NSB + 512], in_=ii))
                    P.run("vector", lambda v: v.reciprocal(
                        out=AF32[0:1, RR:RR + 512], in_=AF32[0:1, NSB:NSB + 512]))
                    P.run("vector", lambda v, b=AF32[0:1, GS2 + i * 512:GS2 + i * 512 + 512]:
                          v.tensor_tensor(out=AF32[0:1, GR:GR + 512],
                                          in0=AF32[0:1, RR:RR + 512], in1=b,
                                          op=ALU.mult))
                    grd = P.run("vector", lambda v: v.tensor_copy(out=fscv[:, 0:2], in_=onesf[:, 0:2]), inc="dve")
                    grd_last = grd
                    P.wait("tensor", "dve", max(grd, pd_free_dve))
                    bcd = P.run("tensor", lambda t: t.matmul(
                        out=pD[:, 0:512], lhsT=onesrow[0:1, :],
                        rhs=AF32[0:1, GR:GR + 512], start=True, stop=True), inc="pe")
                    P.wait("vector", "pe", bcd)
                    P.wait("vector", "lq", ot_free[i])
                    otap = bf(OT + i * 512, 512)
                    P.run("vector", lambda v, o=otap, ii=ps_o[i]:
                          v.tensor_copy(out=o, in_=ii))
                    P.run("vector", lambda v, o=otap, a=otap:
                          v.tensor_tensor(out=o, in0=a, in1=pD[:, 0:512],
                                          op=ALU.mult))
                    oscd = P.run("vector", lambda v: v.tensor_copy(out=fscv[:, 0:2], in_=onesf[:, 0:2]), inc="dve")
                    pd_free_dve = oscd
                    pair_free_dve[i] = oscd
                    blk = q // 4
                    dst = o_scr[blk * (NH * DH) + h * DH:blk * (NH * DH) + (h + 1) * DH,
                                (q % 4) * QT:(q % 4 + 1) * QT]
                    P.wait("sync", "dve", oscd)
                    ot_free[i] = P.run("sync", lambda s, o=dst, ii=otap:
                                       s.dma_start(out=o, in_=ii), inc="lq", k=16)

        P.barrier()

        P.run("gpsimd", lambda g: g.collective_compute(
            "AllGather", ALU.bypass,
            replica_groups=[[0, 1], [2, 3], [4, 5], [6, 7]],
            ins=[o_scr], outs=[o_exch]), inc="cc")

        for e in ("sync", "scalar", "tensor", "vector"):
            P.wait(e, "cc", 1)
        SA, SB, T2, ABSB, LOB = 0, 2048, 4096, 6144, 8192
        AE, WT0, WT1, ABSF = 0, 6144, 8192, 10240

        wt_free = {0: ("act", 0), 1: ("act", 0)}
        psum_scale_free = 0
        tau_prev_dve = 0

        def quant_matrix(mi, wsrc, tdst):
            nonlocal psum_scale_free, tau_prev_dve
            abs_cnt = {}
            ssum_cnt = {}
            lo_cnt = {}
            sgn_cnt = {}
            t2_cnt = {}
            st_cnt = {}
            nrow = wsrc.tensor.shape[0] // 128
            for i in range(64):
                rr_, cc4 = i % nrow, i // nrow
                slot = i % 2
                wto = WT0 if slot == 0 else WT1
                tile = wsrc[rr_ * 128:(rr_ + 1) * 128, cc4 * 2048:(cc4 + 1) * 2048]
                sem, val = wt_free[slot]
                P.wait("sync", sem, val)
                ld = P.run("sync", lambda s, o=f32(wto, 2048), ii=tile:
                           s.dma_start(out=o, in_=ii), inc="lq", k=16)
                P.wait("scalar", "lq", ld)
                if i >= 1:
                    P.wait("scalar", "pe", ssum_cnt[i - 1])
                P.run("scalar", lambda s, o=bf(ABSB, 2048), ii=f32(wto, 2048):
                      s.activation(out=o, in_=ii, func=AF.Abs))
                P.run("scalar", lambda s, o=f32(ABSF, 2048), ii=f32(wto, 2048):
                      s.activation(out=o, in_=ii, func=AF.Abs))
                abs_cnt[i] = P.run("scalar", lambda s: s.activation(out=fsc[:, 0:2], in_=onesf[:, 0:2], func=AF.Square), inc="act")
                wt_free[slot] = ("act", abs_cnt[i])
                P.wait("vector", "act", abs_cnt[i])
                if i >= 1:
                    P.wait("vector", "pe", ssum_cnt[i - 1])
                P.run("vector", lambda v: v.tensor_tensor(
                    out=bf(LOB, 2048), in0=f32(ABSF, 2048), in1=bf(ABSB, 2048),
                    op=ALU.subtract))
                lo_cnt[i] = P.run("vector", lambda v: v.tensor_copy(
                    out=fscv[:, 0:2], in_=onesf[:, 0:2]), inc="dve")
                P.wait("tensor", "act", abs_cnt[i])
                P.wait("tensor", "dve", lo_cnt[i])
                if i == 0:
                    P.wait("tensor", "dve", max(psum_scale_free, tau_prev_dve))
                for j in range(4):
                    psq = (pA if j < 2 else pB)[0:1, (j % 2) * 512:(j % 2) * 512 + 512]
                    P.run("tensor", lambda t, o=psq, l=onesb[:, 0:1],
                          r=bf(ABSB + j * 512, 512), st=(i == 0), sp=(i == 63):
                          t.matmul(out=o, lhsT=l, rhs=r, start=st, stop=sp))
                for j in range(4):
                    psq = (pC if j < 2 else pD)[0:1, (j % 2) * 512:(j % 2) * 512 + 512]
                    ssum_cnt[i] = P.run("tensor", lambda t, o=psq, l=onesb[:, 0:1],
                                        r=bf(LOB + j * 512, 512), st=(i == 0),
                                        sp=(i == 63):
                                        t.matmul(out=o, lhsT=l, rhs=r, start=st, stop=sp),
                                        inc="pe" if j == 3 else None)
            P.wait("vector", "pe", ssum_cnt[63])
            P.run("vector", lambda v: v.tensor_copy(out=AF32[0:1, AE:AE + 1024],
                                                    in_=pA[0:1, :]))
            P.run("vector", lambda v: v.tensor_copy(
                out=AF32[0:1, AE + 1024:AE + 2048], in_=pB[0:1, :]))
            P.run("vector", lambda v: v.tensor_copy(
                out=AF32[0:1, AE + 2048:AE + 3072], in_=pC[0:1, :]))
            pcopy = P.run("vector", lambda v: v.tensor_copy(
                out=AF32[0:1, AE + 3072:AE + 4096], in_=pD[0:1, :]), inc="dve")
            psum_scale_free = pcopy
            P.run("vector", lambda v: v.tensor_tensor(
                out=AF32[0:1, AE:AE + 1024], in0=AF32[0:1, AE:AE + 1024],
                in1=AF32[0:1, AE + 1024:AE + 2048], op=ALU.add))
            P.run("vector", lambda v: v.tensor_tensor(
                out=AF32[0:1, AE + 2048:AE + 3072], in0=AF32[0:1, AE + 2048:AE + 3072],
                in1=AF32[0:1, AE + 3072:AE + 4096], op=ALU.add))
            P.run("vector", lambda v: v.tensor_tensor(
                out=AF32[0:1, AE:AE + 1024], in0=AF32[0:1, AE:AE + 1024],
                in1=AF32[0:1, AE + 2048:AE + 3072], op=ALU.add))
            tcd = P.run("vector", lambda v: v.reduce_sum(
                out=AF32[0:1, AE + 4096:AE + 4097], in_=AF32[0:1, AE:AE + 1024],
                axis=AX.X), inc="dve")
            P.wait("tensor", "dve", max(tcd, tau_prev_dve))
            P.run("tensor", lambda t: t.matmul(
                out=pC[0:1, 0:4], lhsT=AF32[0:1, AE + 4096:AE + 4097],
                rhs=consts[0:1, 0:4], start=True, stop=True))
            tbr = P.run("tensor", lambda t: t.matmul(
                out=pC[0:1, 4:6], lhsT=AF32[0:1, AE + 4096:AE + 4097],
                rhs=consts[0:1, 4:6], start=True, stop=True), inc="pe")
            P.wait("vector", "pe", tbr)
            P.run("vector", lambda v: v.tensor_tensor(
                out=sc_s[0:1, 4 * mi:4 * mi + 4], in0=pC[0:1, 0:4],
                in1=consts[0:1, 6:10], op=ALU.add))
            trd = P.run("vector", lambda v: v.tensor_tensor(
                out=AF32[0:1, AE + 4104:AE + 4106], in0=pC[0:1, 4:6],
                in1=consts[0:1, 10:12], op=ALU.add), inc="dve")
            P.wait("tensor", "dve", trd)
            tbc = P.run("tensor", lambda t: t.matmul(
                out=pC[:, 0:2], lhsT=onesrow[0:1, :],
                rhs=AF32[0:1, AE + 4104:AE + 4106], start=True, stop=True), inc="pe")
            P.wait("vector", "pe", tbc)
            tbcd = P.run("vector", lambda v: v.tensor_copy(out=tau_bc[:, 0:2],
                                                           in_=pC[:, 0:2]), inc="dve")
            tau_prev_dve = tbcd
            for i in range(64):
                rr_, cc4 = i % nrow, i // nrow
                slot = i % 2
                wto = WT0 if slot == 0 else WT1
                tile = wsrc[rr_ * 128:(rr_ + 1) * 128, cc4 * 2048:(cc4 + 1) * 2048]
                sem, val = wt_free[slot]
                P.wait("sync", sem, val)
                ld = P.run("sync", lambda s, o=f32(wto, 2048), ii=tile:
                           s.dma_start(out=o, in_=ii), inc="lq", k=16)
                P.wait("scalar", "lq", ld)
                if i == 0:
                    P.wait("scalar", "dve", tbcd)
                if i >= 1:
                    P.wait("scalar", "dve", t2_cnt[i - 1])
                P.run("scalar", lambda s, o=bf(SA, 2048), ii=f32(wto, 2048):
                      s.activation(out=o, in_=ii, func=AF.Sign, bias=tau_bc[:, 0:1]))
                P.run("scalar", lambda s, o=bf(SB, 2048), ii=f32(wto, 2048):
                      s.activation(out=o, in_=ii, func=AF.Sign, bias=tau_bc[:, 1:2]))
                sgn_cnt[i] = P.run("scalar", lambda s: s.activation(out=fsc[:, 0:2], in_=onesf[:, 0:2], func=AF.Square), inc="act")
                wt_free[slot] = ("act", sgn_cnt[i])
                P.wait("vector", "act", sgn_cnt[i])
                if i >= 1:
                    P.wait("vector", "lq", st_cnt[i - 1])
                P.run("vector", lambda v: v.tensor_tensor(
                    out=bf(T2, 2048), in0=bf(SA, 2048), in1=bf(SB, 2048), op=ALU.add))
                t2_cnt[i] = P.run("vector", lambda v: v.tensor_copy(out=fscv[:, 0:2], in_=onesf[:, 0:2]), inc="dve")
                P.wait("sync", "dve", t2_cnt[i])
                dsttile = tdst[rr_ * 128:(rr_ + 1) * 128, cc4 * 2048:(cc4 + 1) * 2048]
                st_cnt[i] = P.run("sync", lambda s, o=dsttile:
                                  s.dma_start(out=o, in_=bf(T2, 2048)), inc="lq", k=16)

        quant_matrix(0, wupT, tu_scr)
        quant_matrix(1, wgateT, tg_scr)
        quant_matrix(2, wdownT, td_scr)
        sgd = P.run("vector", lambda v: v.tensor_tensor(
            out=sc_s[0:1, 12:14], in0=sc_s[0:1, 5:7], in1=sc_s[0:1, 9:11],
            op=ALU.mult), inc="dve")
        P.wait("tensor", "dve", max(sgd, tau_prev_dve))
        P.run("tensor", lambda t: t.matmul(
            out=pC[:, 0:2], lhsT=onesrow[0:1, :], rhs=sc_s[0:1, 1:3],
            start=True, stop=True))
        sbc = P.run("tensor", lambda t: t.matmul(
            out=pC[:, 2:4], lhsT=onesrow[0:1, :], rhs=sc_s[0:1, 12:14],
            start=True, stop=True), inc="pe")
        P.wait("vector", "pe", sbc)
        P.run("vector", lambda v: v.tensor_copy(out=subc[:, 0:4], in_=pC[:, 0:4]),
              inc="dve")

        P.barrier()

        WOH, OMV, SQ3, R1B = 0, 16384, 32768, 33792
        SE3, XO, TT3, R1 = 0, 4096, 5120, 6144

        par_holder = {}

        P.wait("sync", "cc", 1)

        s2mm = 0
        xh_last = 0
        for grp in range(2):
            gc0 = grp * 1024
            if grp == 1:
                P.wait("sync", "pe", P.cnt["pe"])
            omv_ld = 0
            for r in range(2):
                def mk(s, r=r, grp=grp):
                    if "v" not in par_holder:
                        par_holder["v"] = s.partition_id() % 2
                    par = par_holder["v"]
                    srcap = o_exch[bass.ds((r * 2 + par) * 1024, 1024),
                                   grp * 1024:(grp + 1) * 1024]
                    srcap = srcap.rearrange("(c p) t -> p c t", p=128)
                    return s.dma_start(out=c3(bf(OMV + r * 8 * 1024, 8 * 1024), 8),
                                       in_=srcap)
                omv_ld = P.run("sync", mk, inc="lq", k=16)
            r1d = 0
            r1d_oc = {}
            sq_free_pe = 0
            for half in range(2):
                src = woT[:, half * 1024:(half + 1) * 1024].rearrange(
                    "(c p) m -> p c m", p=128)
                if grp + half > 0:
                    P.wait("gpsimd", "pe", P.cnt["pe"])
                wold = P.run("gpsimd", lambda g, o=c3(bf(WOH, FC * 1024), FC), s=src:
                             g.dma_start(out=o, in_=s), inc="lg", k=16)
                P.wait("tensor", "lg", wold)
                P.wait("tensor", "lq", omv_ld)
                for oc2 in range(8):
                    oc = half * 8 + oc2
                    psw = pA if oc % 2 == 0 else pD
                    P.wait("sync", "dve", r1d)
                    xold = P.run("sync", lambda s, o=f32(XO, 1024),
                                 ii=xownT[oc * 128:(oc + 1) * 128, gc0:gc0 + 1024]:
                                 s.dma_start(out=o, in_=ii), inc="lq", k=16)
                    P.wait("tensor", "dve", max(r1d_oc.get(oc - 2, 0), xh_last))
                    wmm = None
                    for hcg in range(16):
                        for hf in range(2):
                            wmm = P.run("tensor", lambda t,
                                        o=psw[:, hf * 512:hf * 512 + 512],
                                        l=bf(WOH + hcg * 1024 + oc2 * 128, 128),
                                        r=bf(OMV + hcg * 1024 + hf * 512, 512),
                                        st=(hcg == 0), sp=(hcg == 15):
                                        t.matmul(out=o, lhsT=l, rhs=r, start=st, stop=sp),
                                        inc="pe" if (hcg == 15 and hf == 1) else None)
                    P.wait("vector", "pe", wmm)
                    P.wait("vector", "lq", xold)
                    r1sl = f32(R1 + oc * 1024, 1024)
                    P.run("vector", lambda v, o=r1sl, pp=psw:
                          v.tensor_tensor(out=o, in0=pp[:, :], in1=f32(XO, 1024),
                                          op=ALU.add))
                    r1d = P.run("vector", lambda v: v.tensor_copy(out=fscv[:, 0:2], in_=onesf[:, 0:2]), inc="dve")
                    r1d_oc[oc] = r1d
                    P.wait("scalar", "dve", r1d)
                    P.wait("scalar", "pe", sq_free_pe)
                    P.wait("vector", "pe", sq_free_pe)
                    P.run("vector", lambda v, o=bf(R1B, 1024), ii=r1sl:
                          v.tensor_copy(out=o, in_=ii))
                    cst = P.run("vector", lambda v: v.tensor_copy(out=fscv[:, 0:2], in_=onesf[:, 0:2]), inc="dve")
                    P.run("scalar", lambda s, o=bf(SQ3, 1024), ii=r1sl:
                          s.activation(out=o, in_=ii, func=AF.Square))
                    sqd = P.run("scalar", lambda s: s.activation(out=fsc[:, 0:2], in_=onesf[:, 0:2], func=AF.Square), inc="act")
                    P.wait("tensor", "act", sqd)
                    P.wait("tensor", "dve", cst)
                    for hf in range(2):
                        P.run("tensor", lambda t, o=pB[0:1, hf * 512:hf * 512 + 512],
                              l=onesb[:, 0:1], r=bf(R1B + hf * 512, 512),
                              st=(oc == 0), sp=(oc == 15):
                              t.matmul(out=o, lhsT=l, rhs=r, start=st, stop=sp))
                    for hf in range(2):
                        s2mm = P.run("tensor", lambda t,
                                     o=pC[0:1, hf * 512:hf * 512 + 512],
                                     l=onesb[:, 0:1], r=bf(SQ3 + hf * 512, 512),
                                     st=(oc == 0), sp=(oc == 15):
                                     t.matmul(out=o, lhsT=l, rhs=r, start=st, stop=sp),
                                     inc="pe" if hf == 1 else None)
                    sq_free_pe = s2mm
            mu = AF32[0:1, SE3:SE3 + 1024]
            var = AF32[0:1, SE3 + 1024:SE3 + 2048]
            m2 = AF32[0:1, SE3 + 2048:SE3 + 3072]
            rstd = AF32[0:1, SE3 + 3072:SE3 + 4096]
            P.wait("vector", "pe", s2mm)
            P.run("vector", lambda v: v.tensor_copy(out=mu, in_=pB[0:1, :]))
            P.run("vector", lambda v: v.tensor_copy(out=var, in_=pC[0:1, :]))
            P.run("vector", lambda v: v.tensor_scalar(
                out=mu, in0=mu, scalar1=float(1.0 / DM), scalar2=None, op0=ALU.mult))
            P.run("vector", lambda v: v.tensor_scalar(
                out=var, in0=var, scalar1=float(1.0 / DM), scalar2=None, op0=ALU.mult))
            P.run("vector", lambda v: v.tensor_tensor(out=m2, in0=mu, in1=mu,
                                                      op=ALU.mult))
            P.run("vector", lambda v: v.tensor_tensor(
                out=var, in0=var, in1=m2, op=ALU.subtract))
            vard = P.run("vector", lambda v: v.tensor_scalar(
                out=var, in0=var, scalar1=1e-5, scalar2=None, op0=ALU.add), inc="dve")
            P.wait("scalar", "dve", vard)
            P.run("scalar", lambda s: s.activation(out=rstd, in_=var, func=AF.Ln))
            P.run("scalar", lambda s: s.activation(out=rstd, in_=rstd, func=AF.Exp,
                                                   scale=-0.5))
            rsd = P.run("scalar", lambda s: s.activation(out=fsc[:, 0:2], in_=onesf[:, 0:2], func=AF.Square), inc="act")
            P.wait("tensor", "act", rsd)
            for hf in range(2):
                P.run("tensor", lambda t, o=pD[:, hf * 512:hf * 512 + 512],
                      l=onesrow[0:1, :],
                      r=AF32[0:1, SE3 + hf * 512:SE3 + hf * 512 + 512]:
                      t.matmul(out=o, lhsT=l, rhs=r, start=True, stop=True))
            bcd2 = None
            for hf in range(2):
                bcd2 = P.run("tensor", lambda t, o=pB[:, hf * 512:hf * 512 + 512],
                             l=onesrow[0:1, :],
                             r=AF32[0:1, SE3 + 3072 + hf * 512:SE3 + 3072 + hf * 512 + 512]:
                             t.matmul(out=o, lhsT=l, rhs=r, start=True, stop=True),
                             inc="pe" if hf == 1 else None)
            P.wait("vector", "pe", bcd2)
            for oc in range(16):
                r1sl = f32(R1 + oc * 1024, 1024)
                P.run("vector", lambda v, a=r1sl:
                      v.tensor_tensor(out=f32(TT3, 1024), in0=a, in1=pD[:, :],
                                      op=ALU.subtract))
                P.run("vector", lambda v, o=r1sl:
                      v.tensor_tensor(out=o, in0=f32(TT3, 1024), in1=pB[:, :],
                                      op=ALU.mult))
                xhd = P.run("vector", lambda v: v.tensor_copy(out=fscv[:, 0:2],
                                                              in_=onesf[:, 0:2]),
                            inc="dve")
                xh_last = xhd
            P.wait("sync", "dve", xh_last)
            for oc in range(16):
                r1sl = f32(R1 + oc * 1024, 1024)
                P.run("sync", lambda s, o=xhat_scr[oc * 128:(oc + 1) * 128,
                                                   gc0:gc0 + 1024], ii=r1sl:
                      s.dma_start(out=o, in_=ii), inc="lq", k=16)

        P.barrier()

        XH, HB, TU0, TU1, TG0, TG1, TD = 0, 16384, 24576, 28672, 32768, 36864, 40960
        SE5, T5, SIL, ACC = 0, 4096, 5120, 6144

        up_cons = {}
        dmm_last = 0
        hdone = 0
        silu_free = 0
        accd = 0
        od_last = 0
        for grp in range(2):
            gc0 = grp * 1024
            if grp == 1:
                P.wait("sync", "dve", od_last)
            accld = P.run("sync", lambda s, o=c3(f32(ACC, 16384), FC),
                          ii=xhat_scr[:, gc0:gc0 + 1024].rearrange(
                              "(c p) t -> p c t", p=128):
                          s.dma_start(out=o, in_=ii), inc="lq", k=16)
            if grp == 1:
                P.wait("gpsimd", "pe", up_cons[31])
            xhld = P.run("gpsimd", lambda g, o=c3(bf(XH, 16384), FC),
                         ii=xhat_scr[:, gc0:gc0 + 1024].rearrange(
                             "(c p) t -> p c t", p=128):
                         g.dma_start(out=o, in_=ii), inc="lg", k=16)
            P.wait("tensor", "lg", xhld)
            P.wait("vector", "lq", accld)
            for seg in range(8):
                td_done = 0
                for hc in range(8):
                    P.wait("sync", "pe", dmm_last)
                    td_done = P.run(
                        "sync", lambda s, o=bf(TD + hc * 2048, 2048),
                        ii=td_scr[(seg * 8 + hc) * 128:(seg * 8 + hc + 1) * 128, :]:
                        s.dma_start(out=o, in_=ii), inc="lq", k=16)
                for lg in range(4):
                    slot = lg % 2
                    tuo = TU0 if slot == 0 else TU1
                    tgo = TG0 if slot == 0 else TG1
                    h0c = seg * 1024 + lg * 256
                    li = seg * 4 + lg
                    if li >= 2:
                        P.wait("sync", "pe", up_cons[li - 2])
                    P.run("sync", lambda s, o=c3(bf(tuo, 4096), FC),
                          ii=tu_scr[:, h0c:h0c + 256].rearrange(
                              "(c p) m -> p c m", p=128):
                          s.dma_start(out=o, in_=ii), inc="lq", k=16)
                    tgd = P.run("sync", lambda s, o=c3(bf(tgo, 4096), FC),
                                ii=tg_scr[:, h0c:h0c + 256].rearrange(
                                    "(c p) m -> p c m", p=128):
                                s.dma_start(out=o, in_=ii), inc="lq", k=16)
                    P.wait("tensor", "lq", tgd)
                    for hh in range(2):
                        P.wait("tensor", "act", silu_free)
                        P.wait("tensor", "dve", max(hdone, od_last))
                        umm = None
                        for fc in range(FC):
                            for hf in range(2):
                                umm = P.run("tensor", lambda t,
                                            o=pA[:, hf * 512:hf * 512 + 512],
                                            l=bf(tuo + fc * 256 + hh * 128, 128),
                                            r=bf(XH + fc * 1024 + hf * 512, 512),
                                            st=(fc == 0), sp=(fc == FC - 1):
                                            t.matmul(out=o, lhsT=l, rhs=r, start=st,
                                                     stop=sp),
                                            inc="pe" if (fc == FC - 1 and hf == 1)
                                            else None)
                        gmm5 = None
                        for fc in range(FC):
                            for hf in range(2):
                                gmm5 = P.run("tensor", lambda t,
                                             o=pB[:, hf * 512:hf * 512 + 512],
                                             l=bf(tgo + fc * 256 + hh * 128, 128),
                                             r=bf(XH + fc * 1024 + hf * 512, 512),
                                             st=(fc == 0), sp=(fc == FC - 1):
                                             t.matmul(out=o, lhsT=l, rhs=r, start=st,
                                                      stop=sp),
                                             inc="pe" if (fc == FC - 1 and hf == 1)
                                             else None)
                        P.wait("scalar", "pe", umm)
                        P.wait("scalar", "dve", hdone)
                        P.run("scalar", lambda s: s.activation(
                            out=f32(SIL, 1024), in_=pA[:, :], func=AF.Silu,
                            scale=subc[:, 0:1]))
                        silu_free = P.run("scalar", lambda s: s.activation(out=fsc[:, 0:2], in_=onesf[:, 0:2], func=AF.Square), inc="act")
                        P.wait("vector", "act", silu_free)
                        P.wait("vector", "pe", gmm5)
                        hidx = lg * 2 + hh
                        P.run("vector", lambda v, o=bf(HB + hidx * 1024, 1024):
                              v.tensor_tensor(out=o, in0=f32(SIL, 1024),
                                              in1=pB[:, :], op=ALU.mult))
                        hdone = P.run("vector", lambda v: v.tensor_copy(out=fscv[:, 0:2], in_=onesf[:, 0:2]), inc="dve")
                    up_cons[li] = gmm5
                P.wait("tensor", "dve", hdone)
                P.wait("tensor", "lq", td_done)
                for dc in range(16):
                    P.wait("tensor", "dve", accd)
                    dmm = None
                    for hc in range(8):
                        for hf in range(2):
                            dmm = P.run("tensor", lambda t,
                                        o=pC[:, hf * 512:hf * 512 + 512],
                                        l=bf(TD + hc * 2048 + dc * 128, 128),
                                        r=bf(HB + hc * 1024 + hf * 512, 512),
                                        st=(hc == 0), sp=(hc == 7):
                                        t.matmul(out=o, lhsT=l, rhs=r, start=st, stop=sp),
                                        inc="pe" if (hc == 7 and hf == 1) else None)
                    P.wait("vector", "pe", dmm)
                    P.run("vector", lambda v, o=f32(ACC + dc * 1024, 1024),
                          b=f32(ACC + dc * 1024, 1024):
                          v.scalar_tensor_tensor(out=o, in0=pC[:, :],
                                                 scalar=subc[:, 2:3], in1=b,
                                                 op0=ALU.mult, op1=ALU.add))
                    accd = P.run("vector", lambda v: v.tensor_copy(out=fscv[:, 0:2], in_=onesf[:, 0:2]), inc="dve")
                dmm_last = dmm
            mu = AF32[0:1, SE5:SE5 + 1024]
            var = AF32[0:1, SE5 + 1024:SE5 + 2048]
            m2 = AF32[0:1, SE5 + 2048:SE5 + 3072]
            rstd = AF32[0:1, SE5 + 3072:SE5 + 4096]
            sq_free_pe = 0
            s2mm5 = 0
            P.wait("scalar", "dve", accd)
            for oc in range(16):
                asl = f32(ACC + oc * 1024, 1024)
                P.wait("tensor", "dve", accd)
                P.wait("scalar", "pe", sq_free_pe)
                P.wait("vector", "pe", sq_free_pe)
                P.wait("vector", "dve", accd)
                P.run("vector", lambda v, o=bf(HB + 1024, 1024), ii=asl:
                      v.tensor_copy(out=o, in_=ii))
                cst = P.run("vector", lambda v: v.tensor_copy(out=fscv[:, 0:2], in_=onesf[:, 0:2]), inc="dve")
                P.run("scalar", lambda s, o=bf(HB, 1024), ii=asl:
                      s.activation(out=o, in_=ii, func=AF.Square))
                sqd = P.run("scalar", lambda s: s.activation(out=fsc[:, 0:2], in_=onesf[:, 0:2], func=AF.Square), inc="act")
                P.wait("tensor", "act", sqd)
                P.wait("tensor", "dve", cst)
                for hf in range(2):
                    P.run("tensor", lambda t, o=pB[0:1, hf * 512:hf * 512 + 512],
                          l=onesb[:, 0:1], r=bf(HB + 1024 + hf * 512, 512),
                          st=(oc == 0), sp=(oc == 15):
                          t.matmul(out=o, lhsT=l, rhs=r, start=st, stop=sp))
                for hf in range(2):
                    s2mm5 = P.run("tensor", lambda t,
                                  o=pD[0:1, hf * 512:hf * 512 + 512],
                                  l=onesb[:, 0:1], r=bf(HB + hf * 512, 512),
                                  st=(oc == 0), sp=(oc == 15):
                                  t.matmul(out=o, lhsT=l, rhs=r, start=st, stop=sp),
                                  inc="pe" if hf == 1 else None)
                sq_free_pe = s2mm5
            P.wait("vector", "pe", s2mm5)
            P.run("vector", lambda v: v.tensor_copy(out=mu, in_=pB[0:1, :]))
            P.run("vector", lambda v: v.tensor_copy(out=var, in_=pD[0:1, :]))
            P.run("vector", lambda v: v.tensor_scalar(
                out=mu, in0=mu, scalar1=float(1.0 / DM), scalar2=None, op0=ALU.mult))
            P.run("vector", lambda v: v.tensor_scalar(
                out=var, in0=var, scalar1=float(1.0 / DM), scalar2=None, op0=ALU.mult))
            P.run("vector", lambda v: v.tensor_tensor(out=m2, in0=mu, in1=mu,
                                                      op=ALU.mult))
            P.run("vector", lambda v: v.tensor_tensor(
                out=var, in0=var, in1=m2, op=ALU.subtract))
            vard = P.run("vector", lambda v: v.tensor_scalar(
                out=var, in0=var, scalar1=1e-5, scalar2=None, op0=ALU.add), inc="dve")
            P.wait("scalar", "dve", vard)
            P.run("scalar", lambda s: s.activation(out=rstd, in_=var, func=AF.Ln))
            P.run("scalar", lambda s: s.activation(out=rstd, in_=rstd, func=AF.Exp,
                                                   scale=-0.5))
            rsd = P.run("scalar", lambda s: s.activation(out=fsc[:, 0:2], in_=onesf[:, 0:2], func=AF.Square), inc="act")
            P.wait("tensor", "act", rsd)
            for hf in range(2):
                P.run("tensor", lambda t, o=pA[:, hf * 512:hf * 512 + 512],
                      l=onesrow[0:1, :],
                      r=AF32[0:1, SE5 + hf * 512:SE5 + hf * 512 + 512]:
                      t.matmul(out=o, lhsT=l, rhs=r, start=True, stop=True))
            bcd3 = None
            for hf in range(2):
                bcd3 = P.run("tensor", lambda t, o=pB[:, hf * 512:hf * 512 + 512],
                             l=onesrow[0:1, :],
                             r=AF32[0:1, SE5 + 3072 + hf * 512:
                                    SE5 + 3072 + hf * 512 + 512]:
                             t.matmul(out=o, lhsT=l, rhs=r, start=True, stop=True),
                             inc="pe" if hf == 1 else None)
            P.wait("vector", "pe", bcd3)
            for oc in range(16):
                asl = f32(ACC + oc * 1024, 1024)
                P.run("vector", lambda v, a=asl:
                      v.tensor_tensor(out=f32(T5, 1024), in0=a, in1=pA[:, :],
                                      op=ALU.subtract))
                P.run("vector", lambda v, o=asl:
                      v.tensor_tensor(out=o, in0=f32(T5, 1024), in1=pB[:, :],
                                      op=ALU.mult))
                od = P.run("vector", lambda v: v.tensor_copy(out=fscv[:, 0:2],
                                                             in_=onesf[:, 0:2]),
                           inc="dve")
                od_last = od
            P.wait("sync", "dve", od_last)
            for oc in range(16):
                asl = f32(ACC + oc * 1024, 1024)
                P.run("sync", lambda s, o=outT[oc * 128:(oc + 1) * 128, gc0:gc0 + 1024],
                      ii=asl: s.dma_start(out=o, in_=ii), inc="st", k=16)

        P.wait("sync", "st", P.cnt["st"])

        with nc.Block() as block:
            for eng in ENGINES:
                def body(e, eng=eng):
                    for fn in P.streams[eng]:
                        fn(e)
                getattr(block, eng)(body)

    return nc


def _tables():
    t = np.arange(S, dtype=np.float32)
    inv = (1.0 / np.power(10000.0, np.arange(0, DH, 2, dtype=np.float64) / DH)
           ).astype(np.float32)
    freqs = (t[:, None] * inv[None, :]).astype(np.float32)
    cos64 = np.cos(freqs.astype(np.float64)).astype(np.float32)
    sin64 = np.sin(freqs.astype(np.float64)).astype(np.float32)
    cosT = np.concatenate([cos64.T, cos64.T], axis=0)
    sinrT = np.concatenate([-sin64.T, sin64.T], axis=0)
    masks = np.zeros((128, 4 * QT), dtype=np.float32)
    tm = np.arange(128)[:, None]
    ql = np.arange(QT)[None, :]
    for j in range(4):
        masks[:, j * QT:(j + 1) * QT] = np.where(j * 128 + tm <= ql, 0.0, NEG)
    return cosT, sinrT, masks


_NC_CACHE = {}


def kernel(L0_attn_hidden_states, L0_attn_Wq, L0_attn_Wk, L0_attn_Wv, L0_attn_Wo,
           L0_attn_Wkc, L0_attn_Wgslc, L0_attn_Wgswa, L0_ffn_W_up, L0_ffn_W_gate,
           L0_ffn_W_down):
    x = np.asarray(L0_attn_hidden_states, dtype=np.float32)
    Wq = np.asarray(L0_attn_Wq, dtype=np.float32)
    Wk = np.asarray(L0_attn_Wk, dtype=np.float32)
    Wv = np.asarray(L0_attn_Wv, dtype=np.float32)
    Wo = np.asarray(L0_attn_Wo, dtype=np.float32)
    Wgslc = np.asarray(L0_attn_Wgslc, dtype=np.float32)
    Wgswa = np.asarray(L0_attn_Wgswa, dtype=np.float32)
    Wu = np.asarray(L0_ffn_W_up, dtype=np.float32)
    Wg = np.asarray(L0_ffn_W_gate, dtype=np.float32)
    Wd = np.asarray(L0_ffn_W_down, dtype=np.float32)

    cosT, sinrT, masks = _tables()
    onesI = np.zeros((128, 8), dtype=np.float32)
    onesI[:, 0] = 1.0
    onesbI = np.zeros((128, 8), dtype=ml_dtypes.bfloat16)
    onesbI[:, 0] = 1.0
    onesrowI = np.ones((1, 128), dtype=np.float32)
    constsI = np.zeros((1, 16), dtype=np.float32)
    constsI[0, 0] = 1.0 / WELEMS
    constsI[0, 1] = 0.5 / WELEMS
    constsI[0, 4] = -0.5 / WELEMS
    constsI[0, 5] = 0.5 / WELEMS
    constsI[0, 6] = 1e-8
    constsI[0, 7] = 5e-9
    constsI[0, 10] = -5e-9
    constsI[0, 11] = 5e-9

    C = np.ascontiguousarray
    wupT = C(Wu.T)
    wgateT = C(Wg.T)
    wdownT = C(Wd.T)
    woT = C(Wo.T)

    in_maps = []
    for core in range(8):
        be, par = core // 2, core % 2
        heads = slice(par * 8 * DH, (par + 1) * 8 * DH)
        kvh = slice(par * 2 * DH, (par + 1) * 2 * DH)
        wgTl = np.zeros((DM, 64), dtype=np.float32)
        wgTl[:, 0:8] = Wgslc[par * 8:(par + 1) * 8, :].T
        wgTl[:, 32:40] = Wgswa[par * 8:(par + 1) * 8, :].T
        in_maps.append({
            "xT": C(x[be].T),
            "xownT": C(x[be, par * TH:(par + 1) * TH, :].T),
            "wqT": C(Wq[heads, :].T),
            "wkT": C(Wk[kvh, :].T),
            "wvT": C(Wv[kvh, :].T),
            "wgT": wgTl,
            "woT": woT,
            "wupT": wupT,
            "wgateT": wgateT,
            "wdownT": wdownT,
            "cosT": cosT,
            "sinrT": sinrT,
            "masksI": masks,
            "onesI": onesI,
            "onesbI": onesbI,
            "onesrowI": onesrowI,
            "constsI": constsI,
        })

    if "nc" not in _NC_CACHE:
        _NC_CACHE["nc"] = build_program()
    nc = _NC_CACHE["nc"]
    res = run_bass_kernel_spmd(nc, in_maps, core_ids=list(range(8)))
    out = np.empty((4, S, DM), dtype=np.float32)
    for core in range(8):
        be, par = core // 2, core % 2
        out[be, par * TH:(par + 1) * TH, :] = np.asarray(
            res.results[core]["outT"], dtype=np.float32).T
    return out
